# version 29
# baseline (speedup 1.0000x reference)
import math
from contextlib import ExitStack

import numpy as np
import ml_dtypes
import concourse.bass as bass
import concourse.mybir as mybir
from concourse.bass_utils import run_bass_kernel_spmd

F32 = mybir.dt.float32
BF16 = mybir.dt.bfloat16
AF = mybir.ActivationFunctionType
ALU = mybir.AluOpType
AX = mybir.AxisListType

BF = ml_dtypes.bfloat16


def full_dims():
    return dict(S=2048, D=4096, H=32, HKV=8, E=8, R=16, QS=611, QE=2047)


ENGINES = ("pe", "act", "dve", "pool", "sp")
ENG_ATTR = {"pe": "tensor", "act": "scalar", "dve": "vector", "pool": "gpsimd", "sp": "sync"}
QUEUES = {"qS": "sp", "qA": "act", "qP": "pool"}
POOL_K = 10


class Bld:
    def __init__(self, nc, stack):
        self.nc = nc
        self.prog = {e: [] for e in ENGINES}
        self.cnt = {e: 0 for e in ENGINES}
        self.sem = {}
        for e in ENGINES:
            self.sem[e] = stack.enter_context(nc.semaphore(name=f"sem_{e}"))
        self.dma_use = {}
        self.dma_rr = {}
        for q in QUEUES:
            self.dma_rr[q] = 0
            for i in range(POOL_K):
                self.sem[(q, i)] = stack.enter_context(nc.semaphore(name=f"sem_{q}_{i}"))
                self.dma_use[(q, i)] = 0
        self.watermark = {}
        self.last_w = {}
        self.readers = {}

    def _wait(self, eng, sem_key, val):
        if self.watermark.get((eng, sem_key), 0) >= val:
            return
        self.watermark[(eng, sem_key)] = val
        s = self.sem[sem_key]
        self.prog[eng].append(lambda E, s=s, v=val: E.wait_ge(s, v))

    def _deps(self, eng, reads, writes, self_key):
        for b in reads:
            lw = self.last_w.get(b)
            if lw is not None:
                sk, v = lw
                if not (sk == "pe" and self_key == "pe"):
                    self._wait(eng, sk, v)
        for b in writes:
            lw = self.last_w.get(b)
            if lw is not None:
                sk, v = lw
                if not (sk == "pe" and self_key == "pe"):
                    self._wait(eng, sk, v)
            for sk, v in self.readers.get(b, {}).items():
                if not (sk == "pe" and self_key == "pe"):
                    self._wait(eng, sk, v)

    def _update(self, reads, writes, sem_key, val):
        for b in writes:
            self.last_w[b] = (sem_key, val)
            self.readers[b] = {}
        for b in reads:
            d = self.readers.setdefault(b, {})
            if d.get(sem_key, 0) < val:
                d[sem_key] = val

    def op(self, eng, emit, reads=(), writes=()):
        self._deps(eng, reads, writes, eng)
        self.cnt[eng] += 1
        v = self.cnt[eng]
        s = self.sem[eng]
        self.prog[eng].append(lambda E, emit=emit, s=s: emit(E).then_inc(s, 1))
        self._update(reads, writes, eng, v)

    def dma(self, queue, out_ap, in_ap, reads=(), writes=(), transpose=False, **kw):
        eng = QUEUES[queue]
        i = self.dma_rr[queue]
        self.dma_rr[queue] = (i + 1) % POOL_K
        sk = (queue, i)
        self._deps(eng, reads, writes, sk)
        prev = self.dma_use[sk] * 16
        if prev > 0:
            self._wait(eng, sk, prev)
        self.dma_use[sk] += 1
        v = self.dma_use[sk] * 16
        s = self.sem[sk]
        self.prog[eng].append(
            lambda E, o=out_ap, ii=in_ap, s=s, t=transpose, kw=kw: E.dma_start(
                o, ii, transpose=t, **kw
            ).then_inc(s, 16)
        )
        self._update(reads, writes, sk, v)

    def barrier(self):
        cnts = dict(self.cnt)
        uses = dict(self.dma_use)
        for e in ENGINES:
            for e2 in ENGINES:
                if e2 != e and cnts[e2] > 0:
                    self._wait(e, e2, cnts[e2])
            for sk, u in uses.items():
                if u > 0:
                    self._wait(e, sk, u * 16)

    def finalize(self):
        for e in ENGINES:
            if e != "sp" and self.cnt[e] > 0:
                self._wait("sp", e, self.cnt[e])
        for sk, u in self.dma_use.items():
            if u > 0:
                self._wait("sp", sk, u * 16)

    def run(self, block):
        for e in ENGINES:
            ops = self.prog[e]
            if not ops:
                continue
            def body(E, ops=ops):
                for f in ops:
                    f(E)
            getattr(block, ENG_ATTR[e])(body)


def build_program(dims):
    S = dims["S"]; D = dims["D"]; H = dims["H"]; HKV = dims["HKV"]
    E_ = dims["E"]; R = dims["R"]; QS_ = dims["QS"]; QE_ = dims["QE"]
    DH = 128
    KV = HKV * DH
    DC = D // 128
    NT = S // 128
    TG = min(512, S)
    G = S // TG
    GSZ = min(G, 2)
    GP = G // GSZ
    FW = min(512, D)
    REP = H // HKV
    QW = min(512, S)
    QP = S // QW
    W5 = min(512, QW)
    NQM = QW // W5
    KC = S // 128
    LORA_SCALE = 2.0
    nmask = QE_ - QS_

    nc = bass.Bass("TRN2", target_bir_lowering=False)

    def din(name, shape):
        return nc.dram_tensor(name, list(shape), F32, kind="ExternalInput").ap()

    x_d = din("x", (S, D))
    wq_d = din("wq", (D, D)); qs_d = din("qs", (D,))
    wk_d = din("wk", (KV, D)); ks_d = din("ks", (KV,))
    wv_d = din("wv", (KV, D)); vs_d = din("vs", (KV,))
    wo_d = din("wo", (D, D)); os_d = din("os", (D,))
    wf_d = din("wf", (D, D)); bf_d = din("bf", (D,))
    a_d = din("A", (E_ * R, D))
    b_d = din("B", (E_, D, R))
    rw_d = din("rW", (E_, D)); rb_d = din("rb", (E_,))
    out_d = nc.dram_tensor("out", [S, D], F32, kind="ExternalOutput").ap()

    def dscr(name, shape, dt=BF16):
        return nc.dram_tensor(name, list(shape), dt, kind="Internal").ap()

    ln1_tm = dscr("ln1_tm", (S, D))
    ln2_tm = dscr("ln2_tm", (S, D))
    sgnq = dscr("sgnq", (D, D)); sgnk = dscr("sgnk", (KV, D)); sgnv = dscr("sgnv", (KV, D))
    sgno = dscr("sgno", (D, D)); wfb = dscr("wfb", (D, D))
    sgnqT = dscr("sgnqT", (D, D)); sgnkT = dscr("sgnkT", (D, KV)); sgnvT = dscr("sgnvT", (D, KV))
    sgnoT = dscr("sgnoT", (D, D)); wfT = dscr("wfT", (D, D))
    qt_sp = dscr("qt_sp", (D, S))
    kt_sp = dscr("kt_sp", (KV, S))
    vt_sp = dscr("vt_sp", (KV, S))
    ot_sp = dscr("ot_sp", (D, S))
    op_sp = dscr("op_sp", (D, S))
    ff_sp = dscr("ff_sp", (D, S))
    xmid = dscr("xmid", (S, D), F32)
    ab_sp = dscr("ab_sp", (E_ * R, D))
    bb_sp = dscr("bb_sp", (D, E_ * R))

    ident32 = nc.inline_tensor(np.eye(32, dtype=np.float32), name="ident32").ap()
    ident8 = nc.inline_tensor(np.eye(E_, dtype=np.float32), name="ident8").ap()
    ones_f32_128 = nc.inline_tensor(np.ones((128, 128), dtype=np.float32), name="ones128").ap()
    one_f32 = nc.inline_tensor(np.ones((1, 1), dtype=np.float32), name="one11").ap()
    ones_col_bf = nc.inline_tensor(np.ones((128, 1), dtype=BF), name="onescol").ap()
    ones_col_f32 = nc.inline_tensor(np.ones((128, 1), dtype=np.float32), name="onescolf").ap()
    ones_bf_128 = nc.inline_tensor(np.ones((128, 128), dtype=BF), name="onesbf128").ap()
    expm = np.zeros((E_, E_ * R), dtype=np.float32)
    for e in range(E_):
        expm[e, e * R:(e + 1) * R] = 1.0
    exp_d = nc.inline_tensor(expm, name="expmat").ap()
    repm = np.zeros((HKV, 128), dtype=np.float32)
    for kv in range(HKV):
        repm[kv, kv * REP:(kv + 1) * REP] = 1.0
    rep_d = nc.inline_tensor(repm, name="repmat").ap()
    mcols = np.zeros((128, NT), dtype=np.float32)
    for t in range(QS_, QE_):
        mcols[t % 128, t // 128] = 1.0 / nmask
    mask_d = nc.inline_tensor(mcols.astype(BF), name="maskcols").ap()

    stack = ExitStack()
    bld = Bld(nc, stack)

    def sb(name, shape, dt):
        return stack.enter_context(nc.sbuf_tensor(name, list(shape), dt))

    arena = sb("arena", (128, DC * S), BF16)

    def ar(dc):
        return arena[:, dc * S:(dc + 1) * S]

    def arb(dc):
        return ("arena", dc)

    qs_cols = sb("qs_cols", (128, DC), F32)
    ks_cols = sb("ks_cols", (128, KV // 128), F32)
    vs_cols = sb("vs_cols", (128, KV // 128), F32)
    os_cols = sb("os_cols", (128, DC), F32)
    b_cols = sb("b_cols", (128, DC), F32)
    sv_all = sb("sv_all", (128, KV // 128), F32)
    vsc_all = sb("vsc_all", (128, KV // 128), F32)
    ks_ev = sb("ks_ev", (128, KV // 128), F32)
    qs_ev = sb("qs_ev", (128, DC), F32)
    os_ev = sb("os_ev", (128, DC), F32)
    qcols = sb("qcols", (128, DC), F32)
    mask_sb = sb("mask_sb", (128, NT), BF16)
    qmax = sb("qmax", (1, 128), F32)
    kmax = sb("kmax", (1, 128), F32)
    qcm = sb("qcm", (128, 32), F32)
    kcm = sb("kcm", (128, 8), F32)
    nm_row = sb("nm_row", (1, 128), F32)
    nm_cols = sb("nm_cols", (128, 128), F32)
    nm_pcols = sb("nm_pcols", (128, 64), F32)
    route_er = sb("route_er", (128, 1), F32)
    stage = sb("stage", (128, 8 * 512), BF16)
    small = sb("small", (128, 64), F32)
    lnstats = sb("lnstats", (128, 96), F32)
    tiny = sb("tiny", (1, 64), F32)
    ones128_sb = sb("ones128_sb", (128, 128), F32)
    one_sb = sb("one_sb", (1, 1), F32)
    eps_col = sb("eps_col", (128, 1), F32)
    onescol_sb = sb("onescol_sb", (128, 1), BF16)
    onesbf_sb = sb("onesbf_sb", (128, 128), BF16)
    onescol_f32 = sb("onescol_f32", (128, 1), F32)
    id32_sb = sb("id32_sb", (32, 32), F32)
    id8_sb = sb("id8_sb", (E_, E_), F32)
    exp_sb = sb("exp_sb", (E_, E_ * R), F32)
    rep_sb = sb("rep_sb", (HKV, 128), F32)
    scl_tmp = sb("scl_tmp", (32, 128), F32)

    psA = stack.enter_context(nc.psum_tensor("psA", [128, 2048], F32))
    psB = stack.enter_context(nc.psum_tensor("psB", [128, 2048], F32))

    def bank(i, w=512):
        t = psA if i < 4 else psB
        return t[:, (i % 4) * 512:(i % 4) * 512 + w]

    def bankb(i):
        return ("ps", i)

    _tq = [0]

    def qT():
        return "qS"

    def mm(out_ap, lhsT, rhs, reads, writes, start=True, stop=True):
        bld.op("pe", lambda t, o=out_ap, l=lhsT, r=rhs, st=start, sp=stop:
               t.matmul(o, l, r, start=st, stop=sp), reads=reads, writes=writes)

    for dst, src, buf in (
        (ones128_sb, ones_f32_128, "c_ones128"), (one_sb, one_f32, "c_one"),
        (onescol_sb, ones_col_bf, "c_onescol"), (onesbf_sb, ones_bf_128, "c_onesbf"),
        (onescol_f32, ones_col_f32, "c_onescolf"),
        (id32_sb, ident32, "c_id32"),
        (id8_sb, ident8, "c_id8"), (exp_sb, exp_d, "c_exp"),
        (rep_sb, rep_d, "c_rep"), (mask_sb, mask_d, "c_mask"),
    ):
        bld.dma("qS", dst[:], src[:], writes=[buf])

    bld.op("dve", lambda v, o=qmax[:]: v.memset(o, 0.0), writes=["qmax_all"])
    bld.op("dve", lambda v, o=kmax[:]: v.memset(o, 0.0), writes=["kmax_all"])
    bld.op("dve", lambda v, o=eps_col[:]: v.memset(o, 1e-5), writes=["eps_col"])

    def load_scale_cols(src_ap, n, dst, dstbuf, scale_mult=None):
        assert n <= 32
        tb = ("scl_tmp",)
        bld.dma("qS", scl_tmp[0:n, :], src_ap.rearrange("(a b) -> a b", b=128),
                writes=[tb])
        pb = bankb(7)
        bld.op("pe", lambda t, o=bank(7, n), i=scl_tmp[0:n, :], idn=id32_sb[0:n, 0:n]:
               t.transpose(o, i, idn), reads=[tb, "c_id32"], writes=[pb])
        if scale_mult is None:
            bld.op("dve", lambda v, o=dst[:, 0:n], i=bank(7, n): v.tensor_copy(o, i),
                   reads=[pb], writes=[dstbuf])
        else:
            bld.op("dve", lambda v, o=dst[:, 0:n], i=bank(7, n), m=scale_mult:
                   v.tensor_scalar(o, i, m, None, ALU.mult), reads=[pb], writes=[dstbuf])

    load_scale_cols(qs_d, DC, qs_cols, "qs_cols", scale_mult=float(1.0 / math.sqrt(DH)))
    load_scale_cols(ks_d, KV // 128, ks_cols, "ks_cols")
    load_scale_cols(vs_d, KV // 128, vs_cols, "vs_cols")
    load_scale_cols(os_d, DC, os_cols, "os_cols")
    load_scale_cols(bf_d, DC, b_cols, "b_cols")

    NBS = D // 512

    def layer_norm_tile(xbuf, xb, lnbuf, lnb, col):
        par = (col // 8) % 2
        stats = lnstats[:, par * 48:par * 48 + 6 * NBS]
        mv = small[:, col:col + 2]
        rstd = small[:, col + 4:col + 5]; nmr = small[:, col + 5:col + 6]
        c = ("lncol", col)
        for j in range(NBS):
            bld.op("dve", lambda v, o=lnstats[:, par * 48 + 6 * j:par * 48 + 6 * j + 6],
                   i=xbuf[:, j * 512:(j + 1) * 512]: v.bn_stats(o, i),
                   reads=[xb], writes=[c])
        bld.op("dve", lambda v, o=mv, i=stats: v.bn_aggr(o, i), reads=[c], writes=[c])
        bld.op("act", lambda s, o=rstd, i=mv[:, 1:2], b=eps_col[:]:
               s.activation(o, i, AF.Sqrt, bias=b), reads=[c, "eps_col"], writes=[c])
        bld.op("dve", lambda v, o=rstd, i=rstd: v.reciprocal(o, i), reads=[c], writes=[c])
        bld.op("dve", lambda v, o=nmr, i=mv[:, 0:1], j=rstd: v.scalar_tensor_tensor(
            o, i, -1.0, j, ALU.mult, ALU.mult), reads=[c], writes=[c])
        bld.op("act", lambda s, o=lnbuf[:], i=xbuf[:], sc=rstd, b=nmr:
               s.activation(o, i, AF.Identity, bias=b, scale=sc),
               reads=[xb, c], writes=[lnb])

    wbufs = {}
    prep_q = []

    def open_prep_bufs(st, sfx, nbuf=2):
        wbufs["nb"] = nbuf
        wbufs["wld"] = [st.enter_context(nc.sbuf_tensor(f"wld{sfx}{i}", [128, D], BF16))
                        for i in range(nbuf)]
        wbufs["sgn"] = [st.enter_context(nc.sbuf_tensor(f"sgn{sfx}{i}", [128, D], BF16))
                        for i in range(nbuf)]

    def pump(n):
        for _ in range(n):
            if prep_q:
                prep_q.pop(0)[1]()

    def pump_label(label):
        while any(l == label for l, _ in prep_q):
            prep_q.pop(0)[1]()

    def queue_prep(w_dram, sgn_dram, O, wtag, scl_cols=None, sclbuf=None,
                   ev_dst=None, evbuf=None, s_dst=None, s_dstbuf=None):
        def prep_chunk(fc):
            nb = wbufs["nb"]
            wld = wbufs["wld"][fc % nb]; sgn_sb = wbufs["sgn"][fc % nb]
            wb = ("wld", fc % nb)
            bld.dma("qP", wld[:], w_dram[fc * 128:(fc + 1) * 128, :], writes=[wb])
            sc = ("sprep", fc % 4)
            srow = small[:, 16 + (fc % 4):17 + (fc % 4)]
            bld.op("dve", lambda v, o=srow, i_=wld[:]: v.tensor_reduce(
                o, i_, AX.X, ALU.add, apply_absolute_value=True),
                reads=[wb], writes=[sc])
            bld.op("dve", lambda v, o=srow, i_=srow: v.tensor_scalar(
                o, i_, 1.0 / D, 1e-8, ALU.mult, ALU.max), reads=[sc], writes=[sc])
            if scl_cols is not None:
                bld.op("dve", lambda v, o=ev_dst[:, fc:fc + 1], i_=srow,
                       j=scl_cols[:, fc:fc + 1]: v.tensor_mul(o, i_, j),
                       reads=[sc, sclbuf], writes=[(evbuf, fc)])
            elif s_dst is not None:
                bld.op("dve", lambda v, o=s_dst[:, fc:fc + 1], i_=srow:
                       v.tensor_copy(o, i_), reads=[sc], writes=[(s_dstbuf, fc)])
            sg = ("sgn_sb", fc % nb)
            bld.op("act", lambda s, o=sgn_sb[:], i_=wld[:]:
                   s.activation(o, i_, AF.Sign), reads=[wb], writes=[sg])
            bld.dma("qA", sgn_dram[fc * 128:(fc + 1) * 128, :], sgn_sb[:],
                    reads=[sg], writes=[(wtag, fc)])
        for fc in range(O // 128):
            prep_q.append((wtag, lambda fc=fc: prep_chunk(fc)))

    def queue_tpass(sgn_dram, sgnT_dram, O, wtag):
        nall = [(wtag, fc) for fc in range(O // 128)]

        def t_chunk(dc):
            nb = wbufs["nb"]
            sgn_sb = wbufs["sgn"][dc % nb]
            sg = ("sgn_sb", dc % nb)
            bld.dma(qT(), sgn_sb[:, 0:O], sgn_dram[0:O, dc * 128:(dc + 1) * 128],
                    reads=nall, writes=[sg], transpose=True)
            bld.dma("qP", sgnT_dram[dc * 128:(dc + 1) * 128, 0:O], sgn_sb[:, 0:O],
                    reads=[sg], writes=[(wtag, "T", dc)])
        for dc in range(DC):
            prep_q.append((wtag, lambda dc=dc: t_chunk(dc)))

    def queue_cast(w_dram, cast_dram, O, wtag):
        def c_chunk(fc):
            bld.dma("qP", cast_dram[fc * 128:(fc + 1) * 128, :],
                    w_dram[fc * 128:(fc + 1) * 128, :], writes=[(wtag, fc)])
        for fc in range(O // 128):
            prep_q.append((wtag, lambda fc=fc: c_chunk(fc)))

    queue_prep(wk_d, sgnk, KV, "sgnk", scl_cols=ks_cols, sclbuf="ks_cols",
               ev_dst=ks_ev, evbuf="ks_ev")
    queue_prep(wv_d, sgnv, KV, "sgnv", s_dst=sv_all, s_dstbuf="sv_all")
    queue_prep(wq_d, sgnq, D, "sgnq", scl_cols=qs_cols, sclbuf="qs_cols",
               ev_dst=qs_ev, evbuf="qs_ev")
    queue_prep(wo_d, sgno, D, "sgno", scl_cols=os_cols, sclbuf="os_cols",
               ev_dst=os_ev, evbuf="os_ev")
    queue_cast(wf_d, wfb, D, "wfb")

    pa = ExitStack()

    def pal(name, shape, dt):
        return pa.enter_context(nc.sbuf_tensor(name, list(shape), dt))

    xt = [pal(f"xt{i}", (128, D), F32) for i in range(2)]
    lnt = [stage[:, 0:D]]
    open_prep_bufs(pa, "0")

    for tt in range(NT):
        i = tt % 2
        xb = ("xt", i); lnb = ("lnt", 0)
        bld.dma("qS", xt[i][:], x_d[tt * 128:(tt + 1) * 128, :], writes=[xb])
        layer_norm_tile(xt[i], xb, lnt[0], lnb, col=8 * i)
        bld.dma("qA", ln1_tm[tt * 128:(tt + 1) * 128, :], lnt[0], reads=[lnb],
                writes=[("ln1_tm", tt)])
        pump(1)

    HS = S // 2
    for hh in range(2):
        rt_h = [("ln1_tm", t) for t in range(hh * (NT // 2), (hh + 1) * (NT // 2))]
        for dc in range(DC):
            bld.dma(qT(), ar(dc)[:, hh * HS:(hh + 1) * HS],
                    ln1_tm[hh * HS:(hh + 1) * HS, dc * 128:(dc + 1) * 128],
                    reads=rt_h, writes=[arb(dc)], transpose=True)

    bld.barrier()
    pa.close()

    pw = ExitStack()

    def pwl(name, shape, dt):
        return pw.enter_context(nc.sbuf_tensor(name, list(shape), dt))

    def open_proj_bufs(st, sfx):
        wbufs["wT"] = st.enter_context(nc.sbuf_tensor(f"wT{sfx}", [128, DC * FW], BF16))

    open_proj_bufs(pw, "A")

    def proj(stage_dram, O, out_dram, wtag, out_tag, pump_rate=0,
             ev_scales=None, evbuf=None, ev_bias=None, evb_buf=None,
             norm_dst=None, extra_mm=None, g_outer=False, post_gp=None,
             fb_hook=None, pre_mm=None, staged=False):
        wT = wbufs["wT"]
        FWo = min(FW, O)
        nfb = O // FWo
        nfi = FWo // 128
        assert nfi * GSZ <= 8
        pump_label(wtag)
        for sw in range(GP if g_outer else 1):
          for fb in range(nfb):
            fcs = [fb * nfi + fi for fi in range(nfi)]
            for dc in range(DC):
                if staged:
                    bld.dma("qS", wT[:, dc * FW:dc * FW + FWo],
                            stage_dram[dc * 128:(dc + 1) * 128, fb * FWo:(fb + 1) * FWo],
                            reads=[(wtag, "T", dc)],
                            writes=[("wT", dc)])
                else:
                    bld.dma(qT(), wT[:, dc * FW:dc * FW + FWo],
                            stage_dram[fb * FWo:(fb + 1) * FWo, dc * 128:(dc + 1) * 128],
                            reads=[(wtag, fc) for fc in fcs],
                            writes=[("wT", dc)], transpose=True)
            if pre_mm is not None:
                pre_mm(sw, fb)
            for gp in ([sw] if g_outer else range(GP)):
                for dc in range(DC):
                    for fi in range(nfi):
                        for g2 in range(GSZ):
                            g = gp * GSZ + g2
                            bi = fi * GSZ + g2
                            mm(bank(bi, TG),
                               wT[:, dc * FW + fi * 128: dc * FW + fi * 128 + 128],
                               ar(dc)[:, g * TG:(g + 1) * TG],
                               reads=[("wT", dc), arb(dc)], writes=[bankb(bi)],
                               start=(dc == 0),
                               stop=(dc == DC - 1 and extra_mm is None))
                if extra_mm is not None:
                    for fi in range(nfi):
                        for g2 in range(GSZ):
                            extra_mm(fcs[fi], gp * GSZ + g2, bank(fi * GSZ + g2, TG),
                                     bankb(fi * GSZ + g2))
                for fi in range(nfi):
                    fc = fcs[fi]
                    for g2 in range(GSZ):
                        g = gp * GSZ + g2
                        bi = fi * GSZ + g2
                        stg = stage[:, bi * 512:bi * 512 + TG]
                        stgb = ("stage", bi)
                        kw = {}
                        rd = [bankb(bi)]
                        if ev_scales is not None:
                            kw["scale"] = ev_scales[:, fc:fc + 1]
                            rd.append((evbuf, fc))
                        if ev_bias is not None:
                            kw["bias"] = ev_bias[:, fc:fc + 1]
                            rd.append((evb_buf, fc))
                        bld.op("act", lambda s, o=stg, i=bank(bi, TG), kw=kw:
                               s.activation(o, i, AF.Identity, **kw),
                               reads=rd, writes=[stgb])
                        bld.dma("qA",
                                out_dram[fc * 128:(fc + 1) * 128, g * TG:(g + 1) * TG],
                                stg, reads=[stgb], writes=[(out_tag, fc, g)])
                        if norm_dst is not None:
                            dst, dtag = norm_dst
                            if g == 0:
                                bld.op("dve", lambda v, o=dst[:, fc:fc + 1], i=stg:
                                       v.tensor_reduce(o, i, AX.X, ALU.max,
                                                       apply_absolute_value=True),
                                       reads=[stgb], writes=[(dtag, fc)])
                            else:
                                red = small[:, 56 + bi:57 + bi]
                                redb = ("smallred", bi)
                                bld.op("dve", lambda v, o=red, i=stg:
                                       v.tensor_reduce(o, i, AX.X, ALU.max,
                                                       apply_absolute_value=True),
                                       reads=[stgb], writes=[redb])
                                bld.op("dve", lambda v, o=dst[:, fc:fc + 1], i=red,
                                       oo=dst[:, fc:fc + 1]: v.tensor_max(o, i, oo),
                                       reads=[redb, (dtag, fc)], writes=[(dtag, fc)])
            pump(pump_rate)
            if fb_hook is not None:
                fb_hook(sw, fb)
          if g_outer and post_gp is not None:
            post_gp(sw)

    pw_prep = ExitStack()
    open_prep_bufs(pw_prep, "A")
    bld.dma("qP", ab_sp[:, :], a_d[:, :], writes=[("ab_sp",)])
    for e in range(E_):
        for oc in range(D // 128):
            bld.dma("qP", bb_sp[oc * 128:(oc + 1) * 128, e * R:(e + 1) * R],
                    b_d[e, oc * 128:(oc + 1) * 128, :], writes=[("bb", e, oc)])

    proj(sgnk, KV, kt_sp, "sgnk", "kt", pump_rate=10,
         ev_scales=ks_ev, evbuf="ks_ev", norm_dst=(kcm, "kcm"))

    proj(sgnv, KV, vt_sp, "sgnv", "vt", pump_rate=10)
    for c in range(KV // 128):
        bld.op("dve", lambda v, o=vsc_all[:, c:c + 1], i=sv_all[:, c:c + 1],
               j=vs_cols[:, c:c + 1]: v.tensor_mul(o, i, j),
               reads=[("sv_all", c), "vs_cols"], writes=[("vsc_all", c)])

    pump_label("sgno")
    bld.barrier()
    pw_prep.close()

    ktr = [pwl(f"ktr{i}", (128, S), BF16) for i in range(2)]
    vtm = [pwl(f"vtm{i}", (128, NT * 128), BF16) for i in range(2)]
    qh = [pwl(f"qh{i}", (128, QW), BF16) for i in range(4)]
    pt = [pwl(f"pt{i}", (128, 2 * QW), BF16) for i in range(2)]
    rcp = [pwl("rcp0", (128, QW), F32)] * 2
    obuf = [pwl(f"obuf{i}", (128, QW), BF16) for i in range(2)]
    assert REP % 2 == 0 or REP == 1

    def load_kv(kv, ib):
        bld.dma("qP", ktr[ib][:], kt_sp[kv * 128:(kv + 1) * 128, :],
                reads=[("kt", kv, g) for g in range(G)], writes=[("ktr", ib)])
        for tt in range(NT):
            bld.dma(qT(), vtm[ib][:, tt * 128:(tt + 1) * 128],
                    vt_sp[kv * 128:(kv + 1) * 128, tt * 128:(tt + 1) * 128],
                    reads=[("vt", kv, (tt * 128) // TG)],
                    writes=[("vtm", ib)], transpose=True)

    npair = max(1, REP // 2)
    hpg = REP
    NG_ = H // hpg
    ctxs = [(kv, qp, pr) for kv in range(HKV) for qp in range(QP)
            for pr in range(npair)]

    def ctx_heads(ci):
        kv, qp, pr = ctxs[ci]
        hA = kv * REP + 2 * pr
        return [hA, hA + 1] if REP > 1 else [hA]

    def qh_load(ci):
        kv, qp, pr = ctxs[ci]
        for x, h in enumerate(ctx_heads(ci)):
            bi = (ci % 2) * 2 + x
            bld.dma("qS", qh[bi][:],
                    qt_sp[h * 128:(h + 1) * 128, qp * QW:(qp + 1) * QW],
                    reads=[("qt", h, (qp * QW) // TG)], writes=[("qh", bi)])

    def neg_m_group(grp):
        h0 = grp * hpg
        qsq = small[:, 48:48 + hpg]
        ksq = small[:, 52:53]
        bld.op("dve", lambda v, o=qsq, i=qcm[:, h0:h0 + hpg]: v.tensor_mul(o, i, i),
               reads=[("qcm", c) for c in range(h0, h0 + hpg)], writes=["nm_qsq"])
        bld.op("dve", lambda v, o=ksq, i=kcm[:, grp:grp + 1]: v.tensor_mul(o, i, i),
               reads=[("kcm", grp)], writes=["nm_ksq"])
        mm(bank(4)[0:hpg, 0:1], qsq, onescol_f32[:], reads=["nm_qsq", "c_onescolf"],
           writes=[bankb(4)])
        mm(bank(5)[0:1, 0:1], ksq, onescol_f32[:], reads=["nm_ksq", "c_onescolf"],
           writes=[bankb(5)])
        qs_c = small[0:hpg, 53:54]
        ks_r = tiny[0:1, 30:31]
        bld.op("dve", lambda v, o=qs_c, i=bank(4)[0:hpg, 0:1]: v.tensor_copy(o, i),
               reads=[bankb(4)], writes=["nm_qsc"])
        bld.op("dve", lambda v, o=ks_r, i=bank(5)[0:1, 0:1]: v.tensor_copy(o, i),
               reads=[bankb(5)], writes=["nm_ksr"])
        mm(bank(4)[0:1, 0:hpg], qs_c, id32_sb[0:hpg, 0:hpg],
           reads=["nm_qsc", "c_id32"], writes=[bankb(4)])
        qrow = tiny[0:1, 32:32 + hpg]
        bld.op("dve", lambda v, o=qrow, i=bank(4)[0:1, 0:hpg]: v.tensor_copy(o, i),
               reads=[bankb(4)], writes=["nm_qrow"])
        prow = tiny[0:1, 40:40 + npair]
        if REP > 1:
            bld.op("dve", lambda v, o=prow, a=qrow[0:1, 0:hpg:2], b=qrow[0:1, 1:hpg:2]:
                   v.tensor_tensor(o, a, b, ALU.max), reads=["nm_qrow"],
                   writes=["nm_prow"])
        else:
            bld.op("dve", lambda v, o=prow, i=qrow: v.tensor_copy(o, i),
                   reads=["nm_qrow"], writes=["nm_prow"])
        bld.op("dve", lambda v, o=prow, i=prow, sc=ks_r: v.tensor_single_scalar(
            o, i, sc, ALU.mult), reads=["nm_prow", "nm_ksr"], writes=["nm_prow"])
        bld.op("act", lambda s, o=prow, i=prow: s.activation(o, i, AF.Sqrt),
               reads=["nm_prow"], writes=["nm_prow"])
        bld.op("dve", lambda v, o=prow, i=prow: v.tensor_scalar(
            o, i, -1.0, None, ALU.mult), reads=["nm_prow"], writes=["nm_prow"])
        mm(bank(4)[:, 0:npair], ones128_sb[0:1, :], prow,
           reads=["nm_prow", "c_ones128"], writes=[bankb(4)])
        bld.op("dve", lambda v, o=nm_pcols[:, grp * npair:(grp + 1) * npair],
               i=bank(4)[:, 0:npair]: v.tensor_copy(o, i),
               reads=[bankb(4)], writes=["nm_pcols"])

    def attn_ctx(ci):
        kv, qp, pr = ctxs[ci]
        ib = kv % 2
        if ci + 1 < len(ctxs):
            qh_load(ci + 1)
        hs = ctx_heads(ci)

        def st_mm(x, kc):
            bk = x + (kc % 2) * 2
            bi = (ci % 2) * 2 + x
            mm(bank(bk, QW), ktr[ib][:, kc * 128:(kc + 1) * 128],
               qh[bi][:], reads=[("ktr", ib), ("qh", bi)], writes=[bankb(bk)])

        def exp_op(kc):
            par = kc % 2
            nh = len(hs)
            pi = hs[0] // 2 if REP > 1 else hs[0]
            if QW == 512 and nh == 2:
                span = psA[:, par * 1024: par * 1024 + 1024]
                bld.op("act", lambda s, o=pt[par][:, 0:1024], i=span,
                       b=nm_pcols[:, pi:pi + 1]: s.activation(o, i, AF.Exp, bias=b),
                       reads=[bankb(par * 2), bankb(1 + par * 2), "nm_pcols"],
                       writes=[("pt", par)])
            else:
                for x2 in range(nh):
                    bld.op("act", lambda s, o=pt[par][:, x2 * QW:(x2 + 1) * QW],
                           i=bank(x2 + par * 2, QW), b=nm_pcols[:, pi:pi + 1]:
                           s.activation(o, i, AF.Exp, bias=b),
                           reads=[bankb(x2 + par * 2), "nm_pcols"],
                           writes=[("pt", par)])

        def pv_mm(x, kc):
            par = kc % 2
            mm(bank(4 + x, QW), vtm[ib][:, kc * 128:(kc + 1) * 128],
               pt[par][:, x * QW:(x + 1) * QW],
               reads=[("vtm", ib), ("pt", par)], writes=[bankb(4 + x)],
               start=(kc == 0), stop=(kc == KC - 1))

        def den_mm(x, kc):
            par = kc % 2
            mm(bank(6 + x, QW), onesbf_sb[:], pt[par][:, x * QW:(x + 1) * QW],
               reads=[("pt", par), "c_onesbf"], writes=[bankb(6 + x)],
               start=(kc == 0), stop=(kc == KC - 1))

        for kc in range(KC):
            for x in range(len(hs)):
                st_mm(x, kc)
            if kc >= 1:
                for x in range(len(hs)):
                    pv_mm(x, kc - 1)
                    den_mm(x, kc - 1)
            exp_op(kc)
        for x in range(len(hs)):
            pv_mm(x, KC - 1)
            den_mm(x, KC - 1)
        for x, h in enumerate(hs):
            oi = x
            bld.op("dve", lambda v, o=rcp[x][:], i=bank(6 + x, QW):
                   v.reciprocal(o, i), reads=[bankb(6 + x)], writes=[("rcp", 0)])
            bld.op("dve", lambda v, o=obuf[oi][:],
                   i=bank(4 + x, QW), sc=vsc_all[:, kv:kv + 1], jj=rcp[x][:]:
                   v.scalar_tensor_tensor(o, i, sc, jj, ALU.mult, ALU.mult),
                   reads=[bankb(4 + x), ("rcp", 0), ("vsc_all", kv)],
                   writes=[("obuf", oi)])
            bld.dma("qP", ot_sp[h * 128:(h + 1) * 128, qp * QW:(qp + 1) * QW],
                    obuf[oi][:], reads=[("obuf", oi)], writes=[("ot", h, qp)])
        pump(1)

    cpg = QP * npair
    nfi_q = min(FW, D) // 128
    gpf = max(1, nfi_q // hpg)

    load_kv(0, 0)

    def q_hook(sw, fb):
        for g in range(fb * gpf, (fb + 1) * gpf):
            neg_m_group(g)
        for g in range((fb - 1) * gpf, fb * gpf):
            if g >= 0:
                for ci in range(g * cpg, (g + 1) * cpg):
                    if ci == 0:
                        qh_load(0)
                    attn_ctx(ci)
        for g in range(fb * gpf, (fb + 1) * gpf):
            if g + 1 < NG_:
                load_kv(g + 1, (g + 1) % 2)

    proj(sgnq, D, qt_sp, "sgnq", "qt", pump_rate=4,
         ev_scales=qs_ev, evbuf="qs_ev", norm_dst=(qcm, "qcm"),
         fb_hook=q_hook)
    nfb_q = D // min(FW, D)
    for g in range((nfb_q - 1) * gpf, nfb_q * gpf):
        for ci in range(g * cpg, (g + 1) * cpg):
            if ci == 0:
                qh_load(0)
            attn_ctx(ci)

    pump(len(prep_q))
    bld.barrier()
    pw.close()

    pc = ExitStack()
    open_proj_bufs(pc, "O")

    for h in range(H):
        bld.dma("qS", ar(h)[:, :], ot_sp[h * 128:(h + 1) * 128, :],
                reads=[("ot", h, qp) for qp in range(QP)], writes=[arb(h)])

    def pcl(name, shape, dt):
        return pc.enter_context(nc.sbuf_tensor(name, list(shape), dt))

    optm = pcl("optm", (128, D), BF16)
    xt2 = pcl("xt2", (128, D), F32)
    ln2t = pcl("ln2t", (128, D), BF16)

    def c0b_tile(tt):
        ob = ("optm",)
        bld.dma(qT(), optm[:], op_sp[0:D, tt * 128:(tt + 1) * 128],
                reads=[("opj", fc, (tt * 128) // TG) for fc in range(DC)],
                writes=[ob], transpose=True)
        xb = ("xt2",)
        bld.dma("qS", xt2[:], x_d[tt * 128:(tt + 1) * 128, :], writes=[xb])
        bld.op("dve", lambda v, o=xt2[:], a=xt2[:], b=optm[:]:
               v.tensor_add(o, a, b), reads=[xb, ob], writes=[xb])
        bld.dma("qP", xmid[tt * 128:(tt + 1) * 128, :], xt2[:], reads=[xb],
                writes=[("xmid", tt)])
        lnb = ("ln2t",)
        layer_norm_tile(xt2, xb, ln2t, lnb, col=32 + 8 * (tt % 2))
        bld.dma("qA", ln2_tm[tt * 128:(tt + 1) * 128, :], ln2t[:], reads=[lnb],
                writes=[("ln2_tm", tt)])

    tiles_per_gp = GSZ * (TG // 128)

    def emit_router():
        rw_small = scl_tmp[0:E_, :]
        t_first = QS_ // 128
        t_last = (QE_ - 1) // 128
        first = True
        for tt in range(t_first, t_last + 1):
            rb_ = ("optm",)
            bld.dma("qP", optm[:], ln2_tm[tt * 128:(tt + 1) * 128, :],
                    reads=[("ln2_tm", tt)], writes=[rb_])
            for dc in range(DC):
                mm(bank(0)[:, dc:dc + 1], optm[:, dc * 128:(dc + 1) * 128],
                   mask_sb[:, tt:tt + 1], reads=[rb_, "c_mask"], writes=[bankb(0)],
                   start=first, stop=(tt == t_last and dc == DC - 1))
                first = False
        bld.op("dve", lambda v, o=qcols[:, 0:DC], i=bank(0)[:, 0:DC]: v.tensor_copy(o, i),
               reads=[bankb(0)], writes=[("qcols", dc2) for dc2 in range(DC)])

        rwc = small[:, 54:54 + E_]
        for dc in range(DC):
            bld.dma("qS", rw_small, rw_d[:, dc * 128:(dc + 1) * 128], writes=["rw_sb"])
            bld.op("pe", lambda t, o=bank(7)[:, 0:E_], i=rw_small,
                   idn=id8_sb[:]: t.transpose(o, i, idn),
                   reads=["rw_sb", "c_id8"], writes=[bankb(7)])
            bld.op("dve", lambda v, o=rwc, i=bank(7)[:, 0:E_]: v.tensor_copy(o, i),
                   reads=[bankb(7)], writes=["rwc"])
            mm(bank(5)[0:E_, 0:1], rwc, qcols[:, dc:dc + 1],
               reads=["rwc", ("qcols", dc)], writes=[bankb(5)],
               start=(dc == 0), stop=(dc == DC - 1))
        rbrow = tiny[0:1, 0:E_]
        bld.dma("qS", rbrow, rb_d.rearrange("(o e) -> o e", o=1), writes=["rbrow"])
        bld.op("pe", lambda t, o=bank(6)[0:E_, 0:1], l=rbrow, r=one_sb[:]:
               t.matmul(o, l, r, start=True, stop=True), reads=["rbrow", "c_one"],
               writes=[bankb(6)])
        lgc = small[0:E_, 51:52]
        bld.op("dve", lambda v, o=lgc, i=bank(5)[0:E_, 0:1]: v.tensor_copy(o, i),
               reads=[bankb(5)], writes=["lgc"])
        bld.op("dve", lambda v, o=lgc, a=lgc, b=bank(6)[0:E_, 0:1]:
               v.tensor_add(o, a, b), reads=["lgc", bankb(6)], writes=["lgc"])
        bld.op("pe", lambda t, o=bank(6)[0:1, 0:E_], l=lgc, r=id8_sb[:]:
               t.matmul(o, l, r, start=True, stop=True), reads=["lgc", "c_id8"],
               writes=[bankb(6)])
        lgrow = tiny[0:1, 8:8 + E_]
        bld.op("dve", lambda v, o=lgrow, i=bank(6)[0:1, 0:E_]: v.tensor_copy(o, i),
               reads=[bankb(6)], writes=["lgrow"])
        mx = tiny[0:1, 16:17]
        bld.op("dve", lambda v, o=mx, i=lgrow: v.tensor_reduce(o, i, AX.X, ALU.max),
               reads=["lgrow"], writes=["mx"])
        exr = tiny[0:1, 17:17 + E_]
        bld.op("dve", lambda v, o=exr, i=lgrow, s=mx: v.tensor_single_scalar(
            o, i, s, ALU.subtract), reads=["lgrow", "mx"], writes=["exr"])
        bld.op("act", lambda s, o=exr, i=exr: s.activation(o, i, AF.Exp),
               reads=["exr"], writes=["exr"])
        sm = tiny[0:1, 25:26]
        bld.op("dve", lambda v, o=sm, i=exr: v.tensor_reduce(o, i, AX.X, ALU.add),
               reads=["exr"], writes=["sm"])
        bld.op("dve", lambda v, o=sm, i=sm: v.reciprocal(o, i), reads=["sm"], writes=["sm"])
        rrow = tiny[0:1, 26:26 + E_]
        bld.op("dve", lambda v, o=rrow, i=exr, s=sm: v.tensor_scalar(
            o, i, s, LORA_SCALE, ALU.mult, ALU.mult), reads=["exr", "sm"], writes=["rrow"])
        bld.op("pe", lambda t, o=bank(6)[0:E_, 0:1], l=rrow, r=one_sb[:]:
               t.matmul(o, l, r, start=True, stop=True), reads=["rrow", "c_one"],
               writes=[bankb(6)])
        rcol = small[0:E_, 52:53]
        bld.op("dve", lambda v, o=rcol, i=bank(6)[0:E_, 0:1]: v.tensor_copy(o, i),
               reads=[bankb(6)], writes=["rcol"])
        mm(bank(6)[:, 0:1], exp_sb[:], rcol, reads=["c_exp", "rcol"], writes=[bankb(6)])
        bld.op("dve", lambda v, o=route_er[:], i=bank(6)[:, 0:1]: v.tensor_copy(o, i),
               reads=[bankb(6)], writes=["route_er"])



    def mk_hooks(tile_fn, tail=None):
        def fb_hook(sw, fb):
            if sw >= 1 and fb < tiles_per_gp:
                tile_fn((sw - 1) * tiles_per_gp + fb)
        def post_gp(sw):
            if sw == GP - 1:
                for tt in range(sw * tiles_per_gp, (sw + 1) * tiles_per_gp):
                    tile_fn(tt)
                if tail is not None:
                    tail()
        return fb_hook, post_gp

    c0b_fb, c0b_post = mk_hooks(c0b_tile, tail=emit_router)
    proj(sgno, D, op_sp, "sgno", "opj", ev_scales=os_ev, evbuf="os_ev",
         g_outer=True, post_gp=c0b_post, fb_hook=c0b_fb)

    bld.barrier()
    pc.close()

    pe_ = ExitStack()

    def pel(name, shape, dt):
        return pe_.enter_context(nc.sbuf_tensor(name, list(shape), dt))

    bTs = [pel(f"bTs{i}", (128, 512), BF16) for i in range(2)]
    uT = pel("uT", (128, G * 512), BF16)
    wbufs["wT"] = pel("wT3", (128, DC * FW), BF16)
    wbufs.pop("sq", None)
    aTs = [pel(f"aTs{i}", (128, 128), BF16) for i in range(4)]
    fftm = pel("fftm", (128, D), BF16)
    xm2 = pel("xm2", (128, D), F32)

    rt2_all = [("ln2_tm", t) for t in range(NT)]
    for dc in range(DC):
        bld.dma(qT(), ar(dc)[:, :], ln2_tm[0:S, dc * 128:(dc + 1) * 128],
                reads=rt2_all, writes=[arb(dc)], transpose=True)


    for dc in range(DC):
        ai = dc % 4
        bld.dma(qT(), aTs[ai][:], ab_sp[0:E_ * R, dc * 128:(dc + 1) * 128],
                reads=[("ab_sp",)], writes=[("aTs", ai)], transpose=True)
        for g in range(G):
            mm(bank(g, TG), aTs[ai][:], ar(dc)[:, g * TG:(g + 1) * TG],
               reads=[("aTs", ai), arb(dc)], writes=[bankb(g)],
               start=(dc == 0), stop=(dc == DC - 1))
    for g in range(G):
        bld.op("act", lambda s, o=uT[:, g * 512:g * 512 + TG], i=bank(g, TG),
               sc=route_er[:]: s.activation(o, i, AF.Identity, scale=sc),
               reads=[bankb(g), "route_er"], writes=[("uT", g)])

    NFI = min(FW, D) // 128

    def bt_pre_mm(sw, fb):
        bi = fb % 2
        bld.dma(qT(), bTs[bi][:, 0:NFI * 128],
                bb_sp[fb * NFI * 128:(fb + 1) * NFI * 128, 0:E_ * R],
                reads=[("bb", e, oc) for e in range(E_)
                       for oc in range(fb * NFI, (fb + 1) * NFI)],
                writes=[("bTs", bi)], transpose=True)

    def lora_mm(fc, g, bankap, bankbuf):
        bi = (fc // NFI) % 2
        mm(bankap, bTs[bi][:, (fc % NFI) * 128:(fc % NFI + 1) * 128],
           uT[:, g * 512:g * 512 + TG],
           reads=[("bTs", bi), ("uT", g)], writes=[bankbuf], start=False, stop=True)

    def c1e_tile(tt):
        fb_ = ("fftm",)
        bld.dma(qT(), fftm[:], ff_sp[0:D, tt * 128:(tt + 1) * 128],
                reads=[("ff", fc, (tt * 128) // TG) for fc in range(DC)],
                writes=[fb_], transpose=True)
        xb = ("xm2",)
        bld.dma("qS", xm2[:], xmid[tt * 128:(tt + 1) * 128, :],
                reads=[("xmid", tt)], writes=[xb])
        bld.op("dve", lambda v, o=xm2[:], a=xm2[:], b=fftm[:]:
               v.tensor_add(o, a, b), reads=[xb, fb_], writes=[xb])
        bld.dma("qS", out_d[tt * 128:(tt + 1) * 128, :], xm2[:], reads=[xb],
                writes=[("out", tt)])

    c1e_fb, c1e_post = mk_hooks(c1e_tile)
    proj(wfb, D, ff_sp, "wfb", "ff", ev_bias=b_cols, evb_buf="b_cols",
         extra_mm=lora_mm, g_outer=True, post_gp=c1e_post, fb_hook=c1e_fb,
         pre_mm=bt_pre_mm)

    bld.barrier()
    pe_.close()

    bld.finalize()

    with nc.Block() as block:
        bld.run(block)
    stack.close()
    return nc



def make_in_maps(dims, inputs, n=8):
    x = np.ascontiguousarray(np.asarray(inputs["L0_attn_hidden_states"], dtype=np.float32))
    c = lambda k: np.ascontiguousarray(np.asarray(inputs[k], np.float32))
    common = {
        "wq": c("L0_attn_q_weight"), "qs": c("L0_attn_q_scale"),
        "wk": c("L0_attn_k_weight"), "ks": c("L0_attn_k_scale"),
        "wv": c("L0_attn_v_weight"), "vs": c("L0_attn_v_scale"),
        "wo": c("L0_attn_o_weight"), "os": c("L0_attn_o_scale"),
        "wf": c("L0_ffn_W"), "bf": c("L0_ffn_b"),
        "A": np.ascontiguousarray(
            np.asarray(inputs["L0_ffn_A"], np.float32).reshape(
                dims["E"] * dims["R"], dims["D"])),
        "B": c("L0_ffn_B"),
        "rW": c("L0_ffn_router_W"), "rb": c("L0_ffn_router_b"),
    }
    return [dict(common, x=np.ascontiguousarray(x[i])) for i in range(n)]


def kernel(**inputs):
    dims = full_dims()
    nc = build_program(dims)
    in_maps = make_in_maps(dims, inputs)
    res = run_bass_kernel_spmd(nc, in_maps, core_ids=list(range(8)))
    out = np.stack([np.asarray(r["out"], dtype=np.float32) for r in res.results], axis=0)
    return out


# version 30
# speedup vs baseline: 1.1284x; 1.1284x over previous
import math
from contextlib import ExitStack

import numpy as np
import ml_dtypes
import concourse.bass as bass
import concourse.mybir as mybir
from concourse.bass_utils import run_bass_kernel_spmd

F32 = mybir.dt.float32
BF16 = mybir.dt.bfloat16
AF = mybir.ActivationFunctionType
ALU = mybir.AluOpType
AX = mybir.AxisListType

BF = ml_dtypes.bfloat16


def full_dims():
    return dict(S=2048, D=4096, H=32, HKV=8, E=8, R=16, QS=611, QE=2047)


ENGINES = ("pe", "act", "dve", "pool", "sp")
ENG_ATTR = {"pe": "tensor", "act": "scalar", "dve": "vector", "pool": "gpsimd", "sp": "sync"}
QUEUES = {"qS": "sp", "qA": "act", "qP": "pool"}
POOL_K = 10


class Bld:
    def __init__(self, nc, stack):
        self.nc = nc
        self.prog = {e: [] for e in ENGINES}
        self.cnt = {e: 0 for e in ENGINES}
        self.sem = {}
        for e in ENGINES:
            self.sem[e] = stack.enter_context(nc.semaphore(name=f"sem_{e}"))
        self.dma_use = {}
        self.dma_rr = {}
        for q in QUEUES:
            self.dma_rr[q] = 0
            for i in range(POOL_K):
                self.sem[(q, i)] = stack.enter_context(nc.semaphore(name=f"sem_{q}_{i}"))
                self.dma_use[(q, i)] = 0
        self.watermark = {}
        self.last_w = {}
        self.readers = {}

    def _wait(self, eng, sem_key, val):
        if self.watermark.get((eng, sem_key), 0) >= val:
            return
        self.watermark[(eng, sem_key)] = val
        s = self.sem[sem_key]
        self.prog[eng].append(lambda E, s=s, v=val: E.wait_ge(s, v))

    def _deps(self, eng, reads, writes, self_key):
        for b in reads:
            lw = self.last_w.get(b)
            if lw is not None:
                sk, v = lw
                if not (sk == "pe" and self_key == "pe"):
                    self._wait(eng, sk, v)
        for b in writes:
            lw = self.last_w.get(b)
            if lw is not None:
                sk, v = lw
                if not (sk == "pe" and self_key == "pe"):
                    self._wait(eng, sk, v)
            for sk, v in self.readers.get(b, {}).items():
                if not (sk == "pe" and self_key == "pe"):
                    self._wait(eng, sk, v)

    def _update(self, reads, writes, sem_key, val):
        for b in writes:
            self.last_w[b] = (sem_key, val)
            self.readers[b] = {}
        for b in reads:
            d = self.readers.setdefault(b, {})
            if d.get(sem_key, 0) < val:
                d[sem_key] = val

    def op(self, eng, emit, reads=(), writes=()):
        self._deps(eng, reads, writes, eng)
        self.cnt[eng] += 1
        v = self.cnt[eng]
        s = self.sem[eng]
        self.prog[eng].append(lambda E, emit=emit, s=s: emit(E).then_inc(s, 1))
        self._update(reads, writes, eng, v)

    def dma(self, queue, out_ap, in_ap, reads=(), writes=(), transpose=False, **kw):
        eng = QUEUES[queue]
        i = self.dma_rr[queue]
        self.dma_rr[queue] = (i + 1) % POOL_K
        sk = (queue, i)
        self._deps(eng, reads, writes, sk)
        prev = self.dma_use[sk] * 16
        if prev > 0:
            self._wait(eng, sk, prev)
        self.dma_use[sk] += 1
        v = self.dma_use[sk] * 16
        s = self.sem[sk]
        self.prog[eng].append(
            lambda E, o=out_ap, ii=in_ap, s=s, t=transpose, kw=kw: E.dma_start(
                o, ii, transpose=t, **kw
            ).then_inc(s, 16)
        )
        self._update(reads, writes, sk, v)

    def barrier(self):
        cnts = dict(self.cnt)
        uses = dict(self.dma_use)
        for e in ENGINES:
            for e2 in ENGINES:
                if e2 != e and cnts[e2] > 0:
                    self._wait(e, e2, cnts[e2])
            for sk, u in uses.items():
                if u > 0:
                    self._wait(e, sk, u * 16)

    def finalize(self):
        for e in ENGINES:
            if e != "sp" and self.cnt[e] > 0:
                self._wait("sp", e, self.cnt[e])
        for sk, u in self.dma_use.items():
            if u > 0:
                self._wait("sp", sk, u * 16)

    def run(self, block):
        for e in ENGINES:
            ops = self.prog[e]
            if not ops:
                continue
            def body(E, ops=ops):
                for f in ops:
                    f(E)
            getattr(block, ENG_ATTR[e])(body)


def build_program(dims):
    S = dims["S"]; D = dims["D"]; H = dims["H"]; HKV = dims["HKV"]
    E_ = dims["E"]; R = dims["R"]; QS_ = dims["QS"]; QE_ = dims["QE"]
    DH = 128
    KV = HKV * DH
    DC = D // 128
    NT = S // 128
    TG = min(512, S)
    G = S // TG
    GSZ = min(G, 2)
    GP = G // GSZ
    FW = min(512, D)
    REP = H // HKV
    QW = min(512, S)
    QP = S // QW
    W5 = min(512, QW)
    NQM = QW // W5
    KC = S // 128
    LORA_SCALE = 2.0
    nmask = QE_ - QS_

    nc = bass.Bass("TRN2", target_bir_lowering=False)

    def din(name, shape):
        return nc.dram_tensor(name, list(shape), F32, kind="ExternalInput").ap()

    x_d = din("x", (S, D))
    wq_d = din("wq", (D, D)); qs_d = din("qs", (D,))
    wk_d = din("wk", (KV, D)); ks_d = din("ks", (KV,))
    wv_d = din("wv", (KV, D)); vs_d = din("vs", (KV,))
    wo_d = din("wo", (D, D)); os_d = din("os", (D,))
    wf_d = din("wf", (D, D)); bf_d = din("bf", (D,))
    a_d = din("A", (E_ * R, D))
    b_d = din("B", (E_, D, R))
    rw_d = din("rW", (E_, D)); rb_d = din("rb", (E_,))
    out_d = nc.dram_tensor("out", [S, D], F32, kind="ExternalOutput").ap()

    def dscr(name, shape, dt=BF16):
        return nc.dram_tensor(name, list(shape), dt, kind="Internal").ap()

    ln1_tm = dscr("ln1_tm", (S, D))
    ln2_tm = dscr("ln2_tm", (S, D))
    sgnq = dscr("sgnq", (D, D)); sgnk = dscr("sgnk", (KV, D)); sgnv = dscr("sgnv", (KV, D))
    sgno = dscr("sgno", (D, D)); wfb = dscr("wfb", (D, D))
    sgnqT = dscr("sgnqT", (D, D)); sgnkT = dscr("sgnkT", (D, KV)); sgnvT = dscr("sgnvT", (D, KV))
    sgnoT = dscr("sgnoT", (D, D)); wfT = dscr("wfT", (D, D))
    qt_sp = dscr("qt_sp", (D, S))
    kt_sp = dscr("kt_sp", (KV, S))
    vt_sp = dscr("vt_sp", (KV, S))
    op_sp = dscr("op_sp", (D, S))
    ff_sp = dscr("ff_sp", (D, S))
    xmid = dscr("xmid", (S, D), F32)
    ab_sp = dscr("ab_sp", (E_ * R, D))
    bb_sp = dscr("bb_sp", (D, E_ * R))

    ident32 = nc.inline_tensor(np.eye(32, dtype=np.float32), name="ident32").ap()
    ident8 = nc.inline_tensor(np.eye(E_, dtype=np.float32), name="ident8").ap()
    ones_f32_128 = nc.inline_tensor(np.ones((128, 128), dtype=np.float32), name="ones128").ap()
    one_f32 = nc.inline_tensor(np.ones((1, 1), dtype=np.float32), name="one11").ap()
    ones_col_bf = nc.inline_tensor(np.ones((128, 1), dtype=BF), name="onescol").ap()
    ones_col_f32 = nc.inline_tensor(np.ones((128, 1), dtype=np.float32), name="onescolf").ap()
    ones_bf_128 = nc.inline_tensor(np.ones((128, 128), dtype=BF), name="onesbf128").ap()
    expm = np.zeros((E_, E_ * R), dtype=np.float32)
    for e in range(E_):
        expm[e, e * R:(e + 1) * R] = 1.0
    exp_d = nc.inline_tensor(expm, name="expmat").ap()
    repm = np.zeros((HKV, 128), dtype=np.float32)
    for kv in range(HKV):
        repm[kv, kv * REP:(kv + 1) * REP] = 1.0
    rep_d = nc.inline_tensor(repm, name="repmat").ap()
    mcols = np.zeros((128, NT), dtype=np.float32)
    for t in range(QS_, QE_):
        mcols[t % 128, t // 128] = 1.0 / nmask
    mask_d = nc.inline_tensor(mcols.astype(BF), name="maskcols").ap()

    stack = ExitStack()
    bld = Bld(nc, stack)

    def sb(name, shape, dt):
        return stack.enter_context(nc.sbuf_tensor(name, list(shape), dt))

    arena = sb("arena", (128, DC * S), BF16)

    def ar(dc):
        return arena[:, dc * S:(dc + 1) * S]

    def arb(dc):
        return ("arena", dc)

    qs_cols = sb("qs_cols", (128, DC), F32)
    ks_cols = sb("ks_cols", (128, KV // 128), F32)
    vs_cols = sb("vs_cols", (128, KV // 128), F32)
    os_cols = sb("os_cols", (128, DC), F32)
    b_cols = sb("b_cols", (128, DC), F32)
    sv_all = sb("sv_all", (128, KV // 128), F32)
    vsc_all = sb("vsc_all", (128, KV // 128), F32)
    ks_ev = sb("ks_ev", (128, KV // 128), F32)
    qs_ev = sb("qs_ev", (128, DC), F32)
    os_ev = sb("os_ev", (128, DC), F32)
    qcols = sb("qcols", (128, DC), F32)
    mask_sb = sb("mask_sb", (128, NT), BF16)
    qmax = sb("qmax", (1, 128), F32)
    kmax = sb("kmax", (1, 128), F32)
    qcm = sb("qcm", (128, 32), F32)
    kcm = sb("kcm", (128, 8), F32)
    nm_row = sb("nm_row", (1, 128), F32)
    nm_cols = sb("nm_cols", (128, 128), F32)
    route_er = sb("route_er", (128, 1), F32)
    stage = sb("stage", (128, 8 * 512), BF16)
    small = sb("small", (128, 64), F32)
    lnstats = sb("lnstats", (128, 96), F32)
    tiny = sb("tiny", (1, 64), F32)
    ones128_sb = sb("ones128_sb", (128, 128), F32)
    one_sb = sb("one_sb", (1, 1), F32)
    eps_col = sb("eps_col", (128, 1), F32)
    onescol_sb = sb("onescol_sb", (128, 1), BF16)
    onesbf_sb = sb("onesbf_sb", (128, 128), BF16)
    onescol_f32 = sb("onescol_f32", (128, 1), F32)
    id32_sb = sb("id32_sb", (32, 32), F32)
    id8_sb = sb("id8_sb", (E_, E_), F32)
    exp_sb = sb("exp_sb", (E_, E_ * R), F32)
    rep_sb = sb("rep_sb", (HKV, 128), F32)
    scl_tmp = sb("scl_tmp", (32, 128), F32)

    psA = stack.enter_context(nc.psum_tensor("psA", [128, 2048], F32))
    psB = stack.enter_context(nc.psum_tensor("psB", [128, 2048], F32))

    def bank(i, w=512):
        t = psA if i < 4 else psB
        return t[:, (i % 4) * 512:(i % 4) * 512 + w]

    def bankb(i):
        return ("ps", i)

    _tq = [0]

    def qT():
        return "qS"

    def mm(out_ap, lhsT, rhs, reads, writes, start=True, stop=True):
        bld.op("pe", lambda t, o=out_ap, l=lhsT, r=rhs, st=start, sp=stop:
               t.matmul(o, l, r, start=st, stop=sp), reads=reads, writes=writes)

    for dst, src, buf in (
        (ones128_sb, ones_f32_128, "c_ones128"), (one_sb, one_f32, "c_one"),
        (onescol_sb, ones_col_bf, "c_onescol"), (onesbf_sb, ones_bf_128, "c_onesbf"),
        (onescol_f32, ones_col_f32, "c_onescolf"),
        (id32_sb, ident32, "c_id32"),
        (id8_sb, ident8, "c_id8"), (exp_sb, exp_d, "c_exp"),
        (rep_sb, rep_d, "c_rep"), (mask_sb, mask_d, "c_mask"),
    ):
        bld.dma("qS", dst[:], src[:], writes=[buf])

    bld.op("dve", lambda v, o=qmax[:]: v.memset(o, 0.0), writes=["qmax_all"])
    bld.op("dve", lambda v, o=kmax[:]: v.memset(o, 0.0), writes=["kmax_all"])
    bld.op("dve", lambda v, o=eps_col[:]: v.memset(o, 1e-5), writes=["eps_col"])

    def load_scale_cols(src_ap, n, dst, dstbuf, scale_mult=None):
        assert n <= 32
        tb = ("scl_tmp",)
        bld.dma("qS", scl_tmp[0:n, :], src_ap.rearrange("(a b) -> a b", b=128),
                writes=[tb])
        pb = bankb(7)
        bld.op("pe", lambda t, o=bank(7, n), i=scl_tmp[0:n, :], idn=id32_sb[0:n, 0:n]:
               t.transpose(o, i, idn), reads=[tb, "c_id32"], writes=[pb])
        if scale_mult is None:
            bld.op("dve", lambda v, o=dst[:, 0:n], i=bank(7, n): v.tensor_copy(o, i),
                   reads=[pb], writes=[dstbuf])
        else:
            bld.op("dve", lambda v, o=dst[:, 0:n], i=bank(7, n), m=scale_mult:
                   v.tensor_scalar(o, i, m, None, ALU.mult), reads=[pb], writes=[dstbuf])

    load_scale_cols(qs_d, DC, qs_cols, "qs_cols", scale_mult=float(1.0 / math.sqrt(DH)))
    load_scale_cols(ks_d, KV // 128, ks_cols, "ks_cols")
    load_scale_cols(vs_d, KV // 128, vs_cols, "vs_cols")
    load_scale_cols(os_d, DC, os_cols, "os_cols")
    load_scale_cols(bf_d, DC, b_cols, "b_cols")

    NBS = D // 512

    def layer_norm_tile(xbuf, xb, lnbuf, lnb, col):
        par = (col // 8) % 2
        stats = lnstats[:, par * 48:par * 48 + 6 * NBS]
        mv = small[:, col:col + 2]
        rstd = small[:, col + 4:col + 5]; nmr = small[:, col + 5:col + 6]
        c = ("lncol", col)
        for j in range(NBS):
            bld.op("dve", lambda v, o=lnstats[:, par * 48 + 6 * j:par * 48 + 6 * j + 6],
                   i=xbuf[:, j * 512:(j + 1) * 512]: v.bn_stats(o, i),
                   reads=[xb], writes=[c])
        bld.op("dve", lambda v, o=mv, i=stats: v.bn_aggr(o, i), reads=[c], writes=[c])
        bld.op("act", lambda s, o=rstd, i=mv[:, 1:2], b=eps_col[:]:
               s.activation(o, i, AF.Sqrt, bias=b), reads=[c, "eps_col"], writes=[c])
        bld.op("dve", lambda v, o=rstd, i=rstd: v.reciprocal(o, i), reads=[c], writes=[c])
        bld.op("dve", lambda v, o=nmr, i=mv[:, 0:1], j=rstd: v.scalar_tensor_tensor(
            o, i, -1.0, j, ALU.mult, ALU.mult), reads=[c], writes=[c])
        bld.op("act", lambda s, o=lnbuf[:], i=xbuf[:], sc=rstd, b=nmr:
               s.activation(o, i, AF.Identity, bias=b, scale=sc),
               reads=[xb, c], writes=[lnb])

    wbufs = {}
    prep_q = []

    def open_prep_bufs(st, sfx, nbuf=2):
        wbufs["nb"] = nbuf
        wbufs["wld"] = [st.enter_context(nc.sbuf_tensor(f"wld{sfx}{i}", [128, D], BF16))
                        for i in range(nbuf)]
        wbufs["sgn"] = [st.enter_context(nc.sbuf_tensor(f"sgn{sfx}{i}", [128, D], BF16))
                        for i in range(nbuf)]

    def pump(n):
        for _ in range(n):
            if prep_q:
                prep_q.pop(0)[1]()

    def pump_label(label):
        while any(l == label for l, _ in prep_q):
            prep_q.pop(0)[1]()

    def queue_prep(w_dram, sgn_dram, O, wtag, scl_cols=None, sclbuf=None,
                   ev_dst=None, evbuf=None, s_dst=None, s_dstbuf=None):
        def prep_chunk(fc):
            nb = wbufs["nb"]
            wld = wbufs["wld"][fc % nb]; sgn_sb = wbufs["sgn"][fc % nb]
            wb = ("wld", fc % nb)
            bld.dma("qP", wld[:], w_dram[fc * 128:(fc + 1) * 128, :], writes=[wb])
            sc = ("sprep", fc % 4)
            srow = small[:, 16 + (fc % 4):17 + (fc % 4)]
            bld.op("dve", lambda v, o=srow, i_=wld[:]: v.tensor_reduce(
                o, i_, AX.X, ALU.add, apply_absolute_value=True),
                reads=[wb], writes=[sc])
            bld.op("dve", lambda v, o=srow, i_=srow: v.tensor_scalar(
                o, i_, 1.0 / D, 1e-8, ALU.mult, ALU.max), reads=[sc], writes=[sc])
            if scl_cols is not None:
                bld.op("dve", lambda v, o=ev_dst[:, fc:fc + 1], i_=srow,
                       j=scl_cols[:, fc:fc + 1]: v.tensor_mul(o, i_, j),
                       reads=[sc, sclbuf], writes=[(evbuf, fc)])
            elif s_dst is not None:
                bld.op("dve", lambda v, o=s_dst[:, fc:fc + 1], i_=srow:
                       v.tensor_copy(o, i_), reads=[sc], writes=[(s_dstbuf, fc)])
            sg = ("sgn_sb", fc % nb)
            bld.op("act", lambda s, o=sgn_sb[:], i_=wld[:]:
                   s.activation(o, i_, AF.Sign), reads=[wb], writes=[sg])
            bld.dma("qA", sgn_dram[fc * 128:(fc + 1) * 128, :], sgn_sb[:],
                    reads=[sg], writes=[(wtag, fc)])
        for fc in range(O // 128):
            prep_q.append((wtag, lambda fc=fc: prep_chunk(fc)))

    def queue_tpass(sgn_dram, sgnT_dram, O, wtag):
        nall = [(wtag, fc) for fc in range(O // 128)]

        def t_chunk(dc):
            nb = wbufs["nb"]
            sgn_sb = wbufs["sgn"][dc % nb]
            sg = ("sgn_sb", dc % nb)
            bld.dma(qT(), sgn_sb[:, 0:O], sgn_dram[0:O, dc * 128:(dc + 1) * 128],
                    reads=nall, writes=[sg], transpose=True)
            bld.dma("qP", sgnT_dram[dc * 128:(dc + 1) * 128, 0:O], sgn_sb[:, 0:O],
                    reads=[sg], writes=[(wtag, "T", dc)])
        for dc in range(DC):
            prep_q.append((wtag, lambda dc=dc: t_chunk(dc)))

    def queue_cast(w_dram, cast_dram, O, wtag):
        def c_chunk(fc):
            bld.dma("qP", cast_dram[fc * 128:(fc + 1) * 128, :],
                    w_dram[fc * 128:(fc + 1) * 128, :], writes=[(wtag, fc)])
        for fc in range(O // 128):
            prep_q.append((wtag, lambda fc=fc: c_chunk(fc)))

    queue_prep(wk_d, sgnk, KV, "sgnk", scl_cols=ks_cols, sclbuf="ks_cols",
               ev_dst=ks_ev, evbuf="ks_ev")
    queue_prep(wv_d, sgnv, KV, "sgnv", s_dst=sv_all, s_dstbuf="sv_all")
    queue_prep(wq_d, sgnq, D, "sgnq", scl_cols=qs_cols, sclbuf="qs_cols",
               ev_dst=qs_ev, evbuf="qs_ev")
    queue_prep(wo_d, sgno, D, "sgno", scl_cols=os_cols, sclbuf="os_cols",
               ev_dst=os_ev, evbuf="os_ev")
    queue_cast(wf_d, wfb, D, "wfb")

    pa = ExitStack()

    def pal(name, shape, dt):
        return pa.enter_context(nc.sbuf_tensor(name, list(shape), dt))

    xt = [pal(f"xt{i}", (128, D), F32) for i in range(2)]
    lnt = [stage[:, 0:D]]
    open_prep_bufs(pa, "0")

    for tt in range(NT):
        i = tt % 2
        xb = ("xt", i); lnb = ("lnt", 0)
        bld.dma("qS", xt[i][:], x_d[tt * 128:(tt + 1) * 128, :], writes=[xb])
        layer_norm_tile(xt[i], xb, lnt[0], lnb, col=8 * i)
        bld.dma("qA", ln1_tm[tt * 128:(tt + 1) * 128, :], lnt[0], reads=[lnb],
                writes=[("ln1_tm", tt)])
        pump(1)

    HS = S // 2
    for hh in range(2):
        rt_h = [("ln1_tm", t) for t in range(hh * (NT // 2), (hh + 1) * (NT // 2))]
        for dc in range(DC):
            bld.dma(qT(), ar(dc)[:, hh * HS:(hh + 1) * HS],
                    ln1_tm[hh * HS:(hh + 1) * HS, dc * 128:(dc + 1) * 128],
                    reads=rt_h, writes=[arb(dc)], transpose=True)

    bld.barrier()
    pa.close()

    pw = ExitStack()

    def pwl(name, shape, dt):
        return pw.enter_context(nc.sbuf_tensor(name, list(shape), dt))

    def open_proj_bufs(st, sfx):
        wbufs["wT"] = st.enter_context(nc.sbuf_tensor(f"wT{sfx}", [128, DC * FW], BF16))

    open_proj_bufs(pw, "A")

    def proj(stage_dram, O, out_dram, wtag, out_tag, pump_rate=0,
             ev_scales=None, evbuf=None, ev_bias=None, evb_buf=None,
             norm_dst=None, extra_mm=None, g_outer=False, post_gp=None,
             fb_hook=None, pre_mm=None, staged=False):
        wT = wbufs["wT"]
        FWo = min(FW, O)
        nfb = O // FWo
        nfi = FWo // 128
        assert nfi * GSZ <= 8
        pump_label(wtag)
        for sw in range(GP if g_outer else 1):
          for fb in range(nfb):
            fcs = [fb * nfi + fi for fi in range(nfi)]
            for dc in range(DC):
                if staged:
                    bld.dma("qS", wT[:, dc * FW:dc * FW + FWo],
                            stage_dram[dc * 128:(dc + 1) * 128, fb * FWo:(fb + 1) * FWo],
                            reads=[(wtag, "T", dc)],
                            writes=[("wT", dc)])
                else:
                    bld.dma(qT(), wT[:, dc * FW:dc * FW + FWo],
                            stage_dram[fb * FWo:(fb + 1) * FWo, dc * 128:(dc + 1) * 128],
                            reads=[(wtag, fc) for fc in fcs],
                            writes=[("wT", dc)], transpose=True)
            if pre_mm is not None:
                pre_mm(sw, fb)
            for gp in ([sw] if g_outer else range(GP)):
                for dc in range(DC):
                    for fi in range(nfi):
                        for g2 in range(GSZ):
                            g = gp * GSZ + g2
                            bi = fi * GSZ + g2
                            mm(bank(bi, TG),
                               wT[:, dc * FW + fi * 128: dc * FW + fi * 128 + 128],
                               ar(dc)[:, g * TG:(g + 1) * TG],
                               reads=[("wT", dc), arb(dc)], writes=[bankb(bi)],
                               start=(dc == 0),
                               stop=(dc == DC - 1 and extra_mm is None))
                if extra_mm is not None:
                    for fi in range(nfi):
                        for g2 in range(GSZ):
                            extra_mm(fcs[fi], gp * GSZ + g2, bank(fi * GSZ + g2, TG),
                                     bankb(fi * GSZ + g2))
                for fi in range(nfi):
                    fc = fcs[fi]
                    for g2 in range(GSZ):
                        g = gp * GSZ + g2
                        bi = fi * GSZ + g2
                        stg = stage[:, bi * 512:bi * 512 + TG]
                        stgb = ("stage", bi)
                        kw = {}
                        rd = [bankb(bi)]
                        if ev_scales is not None:
                            kw["scale"] = ev_scales[:, fc:fc + 1]
                            rd.append((evbuf, fc))
                        if ev_bias is not None:
                            kw["bias"] = ev_bias[:, fc:fc + 1]
                            rd.append((evb_buf, fc))
                        bld.op("act", lambda s, o=stg, i=bank(bi, TG), kw=kw:
                               s.activation(o, i, AF.Identity, **kw),
                               reads=rd, writes=[stgb])
                        bld.dma("qA",
                                out_dram[fc * 128:(fc + 1) * 128, g * TG:(g + 1) * TG],
                                stg, reads=[stgb], writes=[(out_tag, fc, g)])
                        if norm_dst is not None:
                            dst, dtag = norm_dst
                            if g == 0:
                                bld.op("dve", lambda v, o=dst[:, fc:fc + 1], i=stg:
                                       v.tensor_reduce(o, i, AX.X, ALU.max,
                                                       apply_absolute_value=True),
                                       reads=[stgb], writes=[(dtag, fc)])
                            else:
                                red = small[:, 56 + bi:57 + bi]
                                redb = ("smallred", bi)
                                bld.op("dve", lambda v, o=red, i=stg:
                                       v.tensor_reduce(o, i, AX.X, ALU.max,
                                                       apply_absolute_value=True),
                                       reads=[stgb], writes=[redb])
                                bld.op("dve", lambda v, o=dst[:, fc:fc + 1], i=red,
                                       oo=dst[:, fc:fc + 1]: v.tensor_max(o, i, oo),
                                       reads=[redb, (dtag, fc)], writes=[(dtag, fc)])
            pump(pump_rate)
            if fb_hook is not None:
                fb_hook(sw, fb)
          if g_outer and post_gp is not None:
            post_gp(sw)

    open_prep_bufs(pw, "A")
    proj(sgnk, KV, kt_sp, "sgnk", "kt", pump_rate=10,
         ev_scales=ks_ev, evbuf="ks_ev", norm_dst=(kcm, "kcm"))

    proj(sgnv, KV, vt_sp, "sgnv", "vt", pump_rate=10)
    for c in range(KV // 128):
        bld.op("dve", lambda v, o=vsc_all[:, c:c + 1], i=sv_all[:, c:c + 1],
               j=vs_cols[:, c:c + 1]: v.tensor_mul(o, i, j),
               reads=[("sv_all", c), "vs_cols"], writes=[("vsc_all", c)])

    proj(sgnq, D, qt_sp, "sgnq", "qt", pump_rate=6,
         ev_scales=qs_ev, evbuf="qs_ev", norm_dst=(qcm, "qcm"))

    bld.op("dve", lambda v, o=qcm[:, 0:H], i=qcm[:, 0:H]: v.tensor_mul(o, i, i),
           reads=[("qcm", c) for c in range(H)],
           writes=[("qcm", c) for c in range(H)])
    bld.op("dve", lambda v, o=kcm[:, 0:HKV], i=kcm[:, 0:HKV]: v.tensor_mul(o, i, i),
           reads=[("kcm", c) for c in range(HKV)],
           writes=[("kcm", c) for c in range(HKV)])
    bld.op("pe", lambda t, o=bank(5)[0:H, 0:1], l=qcm[:, 0:H], r=onescol_f32[:]:
           t.matmul(o, l, r, start=True, stop=True),
           reads=[("qcm", c) for c in range(H)] + ["c_onescolf"], writes=[bankb(5)])
    bld.op("pe", lambda t, o=bank(6)[0:HKV, 0:1], l=kcm[:, 0:HKV], r=onescol_f32[:]:
           t.matmul(o, l, r, start=True, stop=True),
           reads=[("kcm", c) for c in range(HKV)] + ["c_onescolf"], writes=[bankb(6)])
    bld.op("pe", lambda t, o=bank(7)[0:1, 0:H], l=bank5_sb[0:H, 0:1], r=id32_sb[0:H, 0:H]:
           t.matmul(o, l, r, start=True, stop=True),
           reads=["bank5_sb", "c_id32"], writes=[bankb(7)]) if False else None
    qsumc = small[0:H, 40:41]
    ksumc = small[0:HKV, 42:43]
    bld.op("dve", lambda v, o=qsumc, i=bank(5)[0:H, 0:1]: v.tensor_copy(o, i),
           reads=[bankb(5)], writes=["qsumc"])
    bld.op("dve", lambda v, o=ksumc, i=bank(6)[0:HKV, 0:1]: v.tensor_copy(o, i),
           reads=[bankb(6)], writes=["ksumc"])
    bld.op("pe", lambda t, o=bank(5)[0:1, 0:H], l=qsumc, r=id32_sb[0:H, 0:H]:
           t.matmul(o, l, r, start=True, stop=True),
           reads=["qsumc", "c_id32"], writes=[bankb(5)])
    bld.op("dve", lambda v, o=qmax[0:1, 0:H], i=bank(5)[0:1, 0:H]: v.tensor_copy(o, i),
           reads=[bankb(5)], writes=["qmax_all"])
    bld.op("pe", lambda t, o=bank(6)[0:1, 0:HKV], l=ksumc, r=id32_sb[0:HKV, 0:HKV]:
           t.matmul(o, l, r, start=True, stop=True),
           reads=["ksumc", "c_id32"], writes=[bankb(6)])
    bld.op("dve", lambda v, o=kmax_col2, i=bank(6)[0:1, 0:HKV]: v.tensor_copy(o, i),
           reads=[bankb(6)], writes=["kmax_col"]) if False else None
    bld.op("dve", lambda v, o=kmax[0:1, 0:HKV], i=bank(6)[0:1, 0:HKV]: v.tensor_copy(o, i),
           reads=[bankb(6)], writes=["kmax_all"])
    kmax_col = small[0:HKV, 44:45]
    bld.op("pe", lambda t, o=bank(6)[0:HKV, 0:1], l=kmax[0:1, 0:HKV], r=one_sb[:]:
           t.matmul(o, l, r, start=True, stop=True),
           reads=["kmax_all", "c_one"], writes=[bankb(6)])
    bld.op("dve", lambda v, o=kmax_col, i=bank(6)[0:HKV, 0:1]: v.tensor_copy(o, i),
           reads=[bankb(6)], writes=["kmax_colb"])
    bld.op("pe", lambda t, o=bank(6)[0:1, 0:128], l=kmax_col, r=rep_sb[:]:
           t.matmul(o, l, r, start=True, stop=True),
           reads=["kmax_colb", "c_rep"], writes=[bankb(6)])
    bld.op("dve", lambda v, o=nm_row[0:1, 0:H], i=qmax[0:1, 0:H],
           j=bank(6)[0:1, 0:H]: v.tensor_mul(o, i, j),
           reads=[bankb(6), "qmax_all"], writes=["nm_row"])
    bld.op("act", lambda s, o=nm_row[0:1, 0:H], i=nm_row[0:1, 0:H]:
           s.activation(o, i, AF.Sqrt), reads=["nm_row"], writes=["nm_row"])
    bld.op("dve", lambda v, o=nm_row[0:1, 0:H], i=nm_row[0:1, 0:H]: v.tensor_scalar(
        o, i, -1.0, None, ALU.mult), reads=["nm_row"], writes=["nm_row"])
    bld.op("pe", lambda t, o=bank(6)[:, 0:H], l=ones128_sb[0:1, :], r=nm_row[0:1, 0:H]:
           t.matmul(o, l, r, start=True, stop=True),
           reads=["nm_row", "c_ones128"], writes=[bankb(6)])
    bld.op("dve", lambda v, o=nm_cols[:, 0:H], i=bank(6)[:, 0:H]: v.tensor_copy(o, i),
           reads=[bankb(6)], writes=["nm_cols"])

    bld.barrier()
    pw.close()

    pb = ExitStack()

    def pbl(name, shape, dt):
        return pb.enter_context(nc.sbuf_tensor(name, list(shape), dt))

    open_prep_bufs(pb, "B")
    bld.dma("qP", ab_sp[:, :], a_d[:, :], writes=[("ab_sp",)])
    for e in range(E_):
        for oc in range(D // 128):
            bld.dma("qP", bb_sp[oc * 128:(oc + 1) * 128, e * R:(e + 1) * R],
                    b_d[e, oc * 128:(oc + 1) * 128, :], writes=[("bb", e, oc)])
    ktr = [pbl(f"ktr{i}", (128, S), BF16) for i in range(2)]
    vtm = [pbl(f"vtm{i}", (128, NT * 128), BF16) for i in range(2)]
    qh = [pbl(f"qh{i}", (128, QW), BF16) for i in range(4)]
    pt = [pbl(f"pt{i}", (128, 2 * QW), BF16) for i in range(2)]
    rcp = [pbl("rcp0", (128, QW), F32)] * 2
    assert REP % 2 == 0 or REP == 1

    def load_kv(kv, ib):
        bld.dma("qP", ktr[ib][:], kt_sp[kv * 128:(kv + 1) * 128, :],
                reads=[("kt", kv, g) for g in range(G)], writes=[("ktr", ib)])
        for tt in range(NT):
            bld.dma(qT(), vtm[ib][:, tt * 128:(tt + 1) * 128],
                    vt_sp[kv * 128:(kv + 1) * 128, tt * 128:(tt + 1) * 128],
                    reads=[("vt", kv, (tt * 128) // TG)],
                    writes=[("vtm", ib)], transpose=True)

    npair = max(1, REP // 2)
    ctxs = [(kv, qp, pr) for kv in range(HKV) for qp in range(QP)
            for pr in range(npair)]

    def ctx_heads(ci):
        kv, qp, pr = ctxs[ci]
        hA = kv * REP + 2 * pr
        return [hA, hA + 1] if REP > 1 else [hA]

    def qh_load(ci):
        kv, qp, pr = ctxs[ci]
        for x, h in enumerate(ctx_heads(ci)):
            bi = (ci % 2) * 2 + x
            bld.dma("qS", qh[bi][:],
                    qt_sp[h * 128:(h + 1) * 128, qp * QW:(qp + 1) * QW],
                    reads=[("qt", h, (qp * QW) // TG)], writes=[("qh", bi)])

    qh_load(0)
    for ci, (kv, qp, pr) in enumerate(ctxs):
        ib = kv % 2
        if qp == 0 and pr == 0:
            if kv == 0:
                load_kv(0, 0)
            if kv + 1 < HKV:
                load_kv(kv + 1, (kv + 1) % 2)
        if ci + 1 < len(ctxs):
            qh_load(ci + 1)
        if True:
            if True:
                hs = ctx_heads(ci)

                def st_mm(x, kc):
                    bk = x * 2 + (kc % 2)
                    bi = (ci % 2) * 2 + x
                    mm(bank(bk, QW), ktr[ib][:, kc * 128:(kc + 1) * 128],
                       qh[bi][:], reads=[("ktr", ib), ("qh", bi)], writes=[bankb(bk)])

                def exp_op(x, kc):
                    bk = x * 2 + (kc % 2)
                    h = hs[x]
                    po = (kc % 2) * QW
                    bld.op("act", lambda s, o=pt[x][:, po:po + QW], i=bank(bk, QW),
                           b=nm_cols[:, h:h + 1]: s.activation(o, i, AF.Exp, bias=b),
                           reads=[bankb(bk), "nm_cols"], writes=[("pt", x, kc % 2)])

                def pv_mm(x, kc):
                    po = (kc % 2) * QW
                    mm(bank(4 + x, QW), vtm[ib][:, kc * 128:(kc + 1) * 128],
                       pt[x][:, po:po + QW],
                       reads=[("vtm", ib), ("pt", x, kc % 2)], writes=[bankb(4 + x)],
                       start=(kc == 0), stop=(kc == KC - 1))

                def den_mm(x, kc):
                    po = (kc % 2) * QW
                    mm(bank(6 + x, QW), onesbf_sb[:], pt[x][:, po:po + QW],
                       reads=[("pt", x, kc % 2), "c_onesbf"], writes=[bankb(6 + x)],
                       start=(kc == 0), stop=(kc == KC - 1))

                for kc in range(KC):
                    for x in range(len(hs)):
                        st_mm(x, kc)
                    if kc >= 1:
                        for x in range(len(hs)):
                            pv_mm(x, kc - 1)
                            den_mm(x, kc - 1)
                    for x in range(len(hs)):
                        exp_op(x, kc)
                for x in range(len(hs)):
                    pv_mm(x, KC - 1)
                    den_mm(x, KC - 1)
                for x, h in enumerate(hs):
                    bld.op("dve", lambda v, o=rcp[x][:], i=bank(6 + x, QW):
                           v.reciprocal(o, i), reads=[bankb(6 + x)], writes=[("rcp", 0)])
                    bld.op("dve",
                           lambda v, o=ar(h)[:, qp * QW:(qp + 1) * QW],
                           i=bank(4 + x, QW), sc=vsc_all[:, kv:kv + 1], jj=rcp[x][:]:
                           v.scalar_tensor_tensor(o, i, sc, jj, ALU.mult, ALU.mult),
                           reads=[bankb(4 + x), ("rcp", 0), ("vsc_all", kv)],
                           writes=[arb(h)])
                pump(2)

    pump(len(prep_q))
    bld.barrier()
    pb.close()

    pc = ExitStack()
    open_proj_bufs(pc, "O")

    def pcl(name, shape, dt):
        return pc.enter_context(nc.sbuf_tensor(name, list(shape), dt))

    optm = pcl("optm", (128, D), BF16)
    xt2 = pcl("xt2", (128, D), F32)
    ln2t = pcl("ln2t", (128, D), BF16)

    def c0b_tile(tt):
        ob = ("optm",)
        bld.dma(qT(), optm[:], op_sp[0:D, tt * 128:(tt + 1) * 128],
                reads=[("opj", fc, (tt * 128) // TG) for fc in range(DC)],
                writes=[ob], transpose=True)
        xb = ("xt2",)
        bld.dma("qS", xt2[:], x_d[tt * 128:(tt + 1) * 128, :], writes=[xb])
        bld.op("dve", lambda v, o=xt2[:], a=xt2[:], b=optm[:]:
               v.tensor_add(o, a, b), reads=[xb, ob], writes=[xb])
        bld.dma("qP", xmid[tt * 128:(tt + 1) * 128, :], xt2[:], reads=[xb],
                writes=[("xmid", tt)])
        lnb = ("ln2t",)
        layer_norm_tile(xt2, xb, ln2t, lnb, col=32 + 8 * (tt % 2))
        bld.dma("qA", ln2_tm[tt * 128:(tt + 1) * 128, :], ln2t[:], reads=[lnb],
                writes=[("ln2_tm", tt)])

    tiles_per_gp = GSZ * (TG // 128)

    def emit_router():
        rw_small = scl_tmp[0:E_, :]
        t_first = QS_ // 128
        t_last = (QE_ - 1) // 128
        first = True
        for tt in range(t_first, t_last + 1):
            rb_ = ("optm",)
            bld.dma("qP", optm[:], ln2_tm[tt * 128:(tt + 1) * 128, :],
                    reads=[("ln2_tm", tt)], writes=[rb_])
            for dc in range(DC):
                mm(bank(0)[:, dc:dc + 1], optm[:, dc * 128:(dc + 1) * 128],
                   mask_sb[:, tt:tt + 1], reads=[rb_, "c_mask"], writes=[bankb(0)],
                   start=first, stop=(tt == t_last and dc == DC - 1))
                first = False
        bld.op("dve", lambda v, o=qcols[:, 0:DC], i=bank(0)[:, 0:DC]: v.tensor_copy(o, i),
               reads=[bankb(0)], writes=[("qcols", dc2) for dc2 in range(DC)])

        rwc = small[:, 54:54 + E_]
        for dc in range(DC):
            bld.dma("qS", rw_small, rw_d[:, dc * 128:(dc + 1) * 128], writes=["rw_sb"])
            bld.op("pe", lambda t, o=bank(7)[:, 0:E_], i=rw_small,
                   idn=id8_sb[:]: t.transpose(o, i, idn),
                   reads=["rw_sb", "c_id8"], writes=[bankb(7)])
            bld.op("dve", lambda v, o=rwc, i=bank(7)[:, 0:E_]: v.tensor_copy(o, i),
                   reads=[bankb(7)], writes=["rwc"])
            mm(bank(5)[0:E_, 0:1], rwc, qcols[:, dc:dc + 1],
               reads=["rwc", ("qcols", dc)], writes=[bankb(5)],
               start=(dc == 0), stop=(dc == DC - 1))
        rbrow = tiny[0:1, 0:E_]
        bld.dma("qS", rbrow, rb_d.rearrange("(o e) -> o e", o=1), writes=["rbrow"])
        bld.op("pe", lambda t, o=bank(6)[0:E_, 0:1], l=rbrow, r=one_sb[:]:
               t.matmul(o, l, r, start=True, stop=True), reads=["rbrow", "c_one"],
               writes=[bankb(6)])
        lgc = small[0:E_, 51:52]
        bld.op("dve", lambda v, o=lgc, i=bank(5)[0:E_, 0:1]: v.tensor_copy(o, i),
               reads=[bankb(5)], writes=["lgc"])
        bld.op("dve", lambda v, o=lgc, a=lgc, b=bank(6)[0:E_, 0:1]:
               v.tensor_add(o, a, b), reads=["lgc", bankb(6)], writes=["lgc"])
        bld.op("pe", lambda t, o=bank(6)[0:1, 0:E_], l=lgc, r=id8_sb[:]:
               t.matmul(o, l, r, start=True, stop=True), reads=["lgc", "c_id8"],
               writes=[bankb(6)])
        lgrow = tiny[0:1, 8:8 + E_]
        bld.op("dve", lambda v, o=lgrow, i=bank(6)[0:1, 0:E_]: v.tensor_copy(o, i),
               reads=[bankb(6)], writes=["lgrow"])
        mx = tiny[0:1, 16:17]
        bld.op("dve", lambda v, o=mx, i=lgrow: v.tensor_reduce(o, i, AX.X, ALU.max),
               reads=["lgrow"], writes=["mx"])
        exr = tiny[0:1, 17:17 + E_]
        bld.op("dve", lambda v, o=exr, i=lgrow, s=mx: v.tensor_single_scalar(
            o, i, s, ALU.subtract), reads=["lgrow", "mx"], writes=["exr"])
        bld.op("act", lambda s, o=exr, i=exr: s.activation(o, i, AF.Exp),
               reads=["exr"], writes=["exr"])
        sm = tiny[0:1, 25:26]
        bld.op("dve", lambda v, o=sm, i=exr: v.tensor_reduce(o, i, AX.X, ALU.add),
               reads=["exr"], writes=["sm"])
        bld.op("dve", lambda v, o=sm, i=sm: v.reciprocal(o, i), reads=["sm"], writes=["sm"])
        rrow = tiny[0:1, 26:26 + E_]
        bld.op("dve", lambda v, o=rrow, i=exr, s=sm: v.tensor_scalar(
            o, i, s, LORA_SCALE, ALU.mult, ALU.mult), reads=["exr", "sm"], writes=["rrow"])
        bld.op("pe", lambda t, o=bank(6)[0:E_, 0:1], l=rrow, r=one_sb[:]:
               t.matmul(o, l, r, start=True, stop=True), reads=["rrow", "c_one"],
               writes=[bankb(6)])
        rcol = small[0:E_, 52:53]
        bld.op("dve", lambda v, o=rcol, i=bank(6)[0:E_, 0:1]: v.tensor_copy(o, i),
               reads=[bankb(6)], writes=["rcol"])
        mm(bank(6)[:, 0:1], exp_sb[:], rcol, reads=["c_exp", "rcol"], writes=[bankb(6)])
        bld.op("dve", lambda v, o=route_er[:], i=bank(6)[:, 0:1]: v.tensor_copy(o, i),
               reads=[bankb(6)], writes=["route_er"])



    def mk_hooks(tile_fn, tail=None):
        def fb_hook(sw, fb):
            if sw >= 1 and fb < tiles_per_gp:
                tile_fn((sw - 1) * tiles_per_gp + fb)
        def post_gp(sw):
            if sw == GP - 1:
                for tt in range(sw * tiles_per_gp, (sw + 1) * tiles_per_gp):
                    tile_fn(tt)
                if tail is not None:
                    tail()
        return fb_hook, post_gp

    c0b_fb, c0b_post = mk_hooks(c0b_tile, tail=emit_router)
    proj(sgno, D, op_sp, "sgno", "opj", ev_scales=os_ev, evbuf="os_ev",
         g_outer=True, post_gp=c0b_post, fb_hook=c0b_fb)

    rt2_all = [("ln2_tm", t) for t in range(NT)]
    for dc in range(DC):
        bld.dma(qT(), ar(dc)[:, :], ln2_tm[0:S, dc * 128:(dc + 1) * 128],
                reads=rt2_all, writes=[arb(dc)], transpose=True)

    bld.barrier()
    pc.close()

    pe_ = ExitStack()

    def pel(name, shape, dt):
        return pe_.enter_context(nc.sbuf_tensor(name, list(shape), dt))

    bTs = [pel(f"bTs{i}", (128, 512), BF16) for i in range(2)]
    uT = pel("uT", (128, G * 512), BF16)
    wbufs["wT"] = pel("wT3", (128, DC * FW), BF16)
    wbufs.pop("sq", None)
    aTs = [pel(f"aTs{i}", (128, 128), BF16) for i in range(4)]
    fftm = pel("fftm", (128, D), BF16)
    xm2 = pel("xm2", (128, D), F32)



    for dc in range(DC):
        ai = dc % 4
        bld.dma(qT(), aTs[ai][:], ab_sp[0:E_ * R, dc * 128:(dc + 1) * 128],
                reads=[("ab_sp",)], writes=[("aTs", ai)], transpose=True)
        for g in range(G):
            mm(bank(g, TG), aTs[ai][:], ar(dc)[:, g * TG:(g + 1) * TG],
               reads=[("aTs", ai), arb(dc)], writes=[bankb(g)],
               start=(dc == 0), stop=(dc == DC - 1))
    for g in range(G):
        bld.op("act", lambda s, o=uT[:, g * 512:g * 512 + TG], i=bank(g, TG),
               sc=route_er[:]: s.activation(o, i, AF.Identity, scale=sc),
               reads=[bankb(g), "route_er"], writes=[("uT", g)])

    NFI = min(FW, D) // 128

    def bt_pre_mm(sw, fb):
        bi = fb % 2
        bld.dma(qT(), bTs[bi][:, 0:NFI * 128],
                bb_sp[fb * NFI * 128:(fb + 1) * NFI * 128, 0:E_ * R],
                reads=[("bb", e, oc) for e in range(E_)
                       for oc in range(fb * NFI, (fb + 1) * NFI)],
                writes=[("bTs", bi)], transpose=True)

    def lora_mm(fc, g, bankap, bankbuf):
        bi = (fc // NFI) % 2
        mm(bankap, bTs[bi][:, (fc % NFI) * 128:(fc % NFI + 1) * 128],
           uT[:, g * 512:g * 512 + TG],
           reads=[("bTs", bi), ("uT", g)], writes=[bankbuf], start=False, stop=True)

    def c1e_tile(tt):
        fb_ = ("fftm",)
        bld.dma(qT(), fftm[:], ff_sp[0:D, tt * 128:(tt + 1) * 128],
                reads=[("ff", fc, (tt * 128) // TG) for fc in range(DC)],
                writes=[fb_], transpose=True)
        xb = ("xm2",)
        bld.dma("qS", xm2[:], xmid[tt * 128:(tt + 1) * 128, :],
                reads=[("xmid", tt)], writes=[xb])
        bld.op("dve", lambda v, o=xm2[:], a=xm2[:], b=fftm[:]:
               v.tensor_add(o, a, b), reads=[xb, fb_], writes=[xb])
        bld.dma("qS", out_d[tt * 128:(tt + 1) * 128, :], xm2[:], reads=[xb],
                writes=[("out", tt)])

    c1e_fb, c1e_post = mk_hooks(c1e_tile)
    proj(wfb, D, ff_sp, "wfb", "ff", ev_bias=b_cols, evb_buf="b_cols",
         extra_mm=lora_mm, g_outer=True, post_gp=c1e_post, fb_hook=c1e_fb,
         pre_mm=bt_pre_mm)

    bld.barrier()
    pe_.close()

    bld.finalize()

    with nc.Block() as block:
        bld.run(block)
    stack.close()
    return nc



def make_in_maps(dims, inputs, n=8):
    x = np.ascontiguousarray(np.asarray(inputs["L0_attn_hidden_states"], dtype=np.float32))
    c = lambda k: np.ascontiguousarray(np.asarray(inputs[k], np.float32))
    common = {
        "wq": c("L0_attn_q_weight"), "qs": c("L0_attn_q_scale"),
        "wk": c("L0_attn_k_weight"), "ks": c("L0_attn_k_scale"),
        "wv": c("L0_attn_v_weight"), "vs": c("L0_attn_v_scale"),
        "wo": c("L0_attn_o_weight"), "os": c("L0_attn_o_scale"),
        "wf": c("L0_ffn_W"), "bf": c("L0_ffn_b"),
        "A": np.ascontiguousarray(
            np.asarray(inputs["L0_ffn_A"], np.float32).reshape(
                dims["E"] * dims["R"], dims["D"])),
        "B": c("L0_ffn_B"),
        "rW": c("L0_ffn_router_W"), "rb": c("L0_ffn_router_b"),
    }
    return [dict(common, x=np.ascontiguousarray(x[i])) for i in range(n)]


def kernel(**inputs):
    dims = full_dims()
    nc = build_program(dims)
    in_maps = make_in_maps(dims, inputs)
    res = run_bass_kernel_spmd(nc, in_maps, core_ids=list(range(8)))
    out = np.stack([np.asarray(r["out"], dtype=np.float32) for r in res.results], axis=0)
    return out


# version 31
# speedup vs baseline: 1.1622x; 1.0299x over previous
import math
from contextlib import ExitStack

import numpy as np
import ml_dtypes
import concourse.bass as bass
import concourse.mybir as mybir
from concourse.bass_utils import run_bass_kernel_spmd

F32 = mybir.dt.float32
BF16 = mybir.dt.bfloat16
AF = mybir.ActivationFunctionType
ALU = mybir.AluOpType
AX = mybir.AxisListType

BF = ml_dtypes.bfloat16


def full_dims():
    return dict(S=2048, D=4096, H=32, HKV=8, E=8, R=16, QS=611, QE=2047)


ENGINES = ("pe", "act", "dve", "pool", "sp")
ENG_ATTR = {"pe": "tensor", "act": "scalar", "dve": "vector", "pool": "gpsimd", "sp": "sync"}
QUEUES = {"qS": "sp", "qA": "act", "qP": "pool"}
POOL_K = 16


class Bld:
    def __init__(self, nc, stack):
        self.nc = nc
        self.prog = {e: [] for e in ENGINES}
        self.cnt = {e: 0 for e in ENGINES}
        self.sem = {}
        for e in ENGINES:
            self.sem[e] = stack.enter_context(nc.semaphore(name=f"sem_{e}"))
        self.dma_use = {}
        self.dma_rr = {}
        for q in QUEUES:
            self.dma_rr[q] = 0
            for i in range(POOL_K):
                self.sem[(q, i)] = stack.enter_context(nc.semaphore(name=f"sem_{q}_{i}"))
                self.dma_use[(q, i)] = 0
        self.watermark = {}
        self.last_w = {}
        self.readers = {}

    def _wait(self, eng, sem_key, val):
        if self.watermark.get((eng, sem_key), 0) >= val:
            return
        self.watermark[(eng, sem_key)] = val
        s = self.sem[sem_key]
        self.prog[eng].append(lambda E, s=s, v=val: E.wait_ge(s, v))

    def _deps(self, eng, reads, writes, self_key):
        for b in reads:
            lw = self.last_w.get(b)
            if lw is not None:
                sk, v = lw
                if not (sk == "pe" and self_key == "pe"):
                    self._wait(eng, sk, v)
        for b in writes:
            lw = self.last_w.get(b)
            if lw is not None:
                sk, v = lw
                if not (sk == "pe" and self_key == "pe"):
                    self._wait(eng, sk, v)
            for sk, v in self.readers.get(b, {}).items():
                if not (sk == "pe" and self_key == "pe"):
                    self._wait(eng, sk, v)

    def _update(self, reads, writes, sem_key, val):
        for b in writes:
            self.last_w[b] = (sem_key, val)
            self.readers[b] = {}
        for b in reads:
            d = self.readers.setdefault(b, {})
            if d.get(sem_key, 0) < val:
                d[sem_key] = val

    def op(self, eng, emit, reads=(), writes=()):
        self._deps(eng, reads, writes, eng)
        self.cnt[eng] += 1
        v = self.cnt[eng]
        s = self.sem[eng]
        self.prog[eng].append(lambda E, emit=emit, s=s: emit(E).then_inc(s, 1))
        self._update(reads, writes, eng, v)

    def dma(self, queue, out_ap, in_ap, reads=(), writes=(), transpose=False, **kw):
        eng = QUEUES[queue]
        i = self.dma_rr[queue]
        self.dma_rr[queue] = (i + 1) % POOL_K
        sk = (queue, i)
        self._deps(eng, reads, writes, sk)
        prev = self.dma_use[sk] * 16
        if prev > 0:
            self._wait(eng, sk, prev)
        self.dma_use[sk] += 1
        v = self.dma_use[sk] * 16
        s = self.sem[sk]
        self.prog[eng].append(
            lambda E, o=out_ap, ii=in_ap, s=s, t=transpose, kw=kw: E.dma_start(
                o, ii, transpose=t, **kw
            ).then_inc(s, 16)
        )
        self._update(reads, writes, sk, v)

    def barrier(self):
        cnts = dict(self.cnt)
        uses = dict(self.dma_use)
        for e in ENGINES:
            for e2 in ENGINES:
                if e2 != e and cnts[e2] > 0:
                    self._wait(e, e2, cnts[e2])
            for sk, u in uses.items():
                if u > 0:
                    self._wait(e, sk, u * 16)

    def finalize(self):
        for e in ENGINES:
            if e != "sp" and self.cnt[e] > 0:
                self._wait("sp", e, self.cnt[e])
        for sk, u in self.dma_use.items():
            if u > 0:
                self._wait("sp", sk, u * 16)

    def run(self, block):
        for e in ENGINES:
            ops = self.prog[e]
            if not ops:
                continue
            def body(E, ops=ops):
                for f in ops:
                    f(E)
            getattr(block, ENG_ATTR[e])(body)


def build_program(dims):
    S = dims["S"]; D = dims["D"]; H = dims["H"]; HKV = dims["HKV"]
    E_ = dims["E"]; R = dims["R"]; QS_ = dims["QS"]; QE_ = dims["QE"]
    DH = 128
    KV = HKV * DH
    DC = D // 128
    NT = S // 128
    TG = min(512, S)
    G = S // TG
    GSZ = min(G, 2)
    GP = G // GSZ
    FW = min(512, D)
    REP = H // HKV
    QW = min(512, S)
    QP = S // QW
    W5 = min(512, QW)
    NQM = QW // W5
    KC = S // 128
    LORA_SCALE = 2.0
    nmask = QE_ - QS_

    nc = bass.Bass("TRN2", target_bir_lowering=False)

    def din(name, shape):
        return nc.dram_tensor(name, list(shape), F32, kind="ExternalInput").ap()

    x_d = din("x", (S, D))
    wq_d = din("wq", (D, D)); qs_d = din("qs", (D,))
    wk_d = din("wk", (KV, D)); ks_d = din("ks", (KV,))
    wv_d = din("wv", (KV, D)); vs_d = din("vs", (KV,))
    wo_d = din("wo", (D, D)); os_d = din("os", (D,))
    wf_d = din("wf", (D, D)); bf_d = din("bf", (D,))
    a_d = din("A", (E_ * R, D))
    b_d = din("B", (E_, D, R))
    rw_d = din("rW", (E_, D)); rb_d = din("rb", (E_,))
    out_d = nc.dram_tensor("out", [S, D], F32, kind="ExternalOutput").ap()

    def dscr(name, shape, dt=BF16):
        return nc.dram_tensor(name, list(shape), dt, kind="Internal").ap()

    ln1_tm = dscr("ln1_tm", (S, D))
    ln2_tm = dscr("ln2_tm", (S, D))
    sgnq = dscr("sgnq", (D, D)); sgnk = dscr("sgnk", (KV, D)); sgnv = dscr("sgnv", (KV, D))
    sgno = dscr("sgno", (D, D)); wfb = dscr("wfb", (D, D))
    sgnqT = dscr("sgnqT", (D, D)); sgnkT = dscr("sgnkT", (D, KV)); sgnvT = dscr("sgnvT", (D, KV))
    sgnoT = dscr("sgnoT", (D, D)); wfT = dscr("wfT", (D, D))
    qt_sp = dscr("qt_sp", (D, S))
    kt_sp = dscr("kt_sp", (KV, S))
    vt_sp = dscr("vt_sp", (KV, S))
    op_sp = dscr("op_sp", (D, S))
    ff_sp = dscr("ff_sp", (D, S))
    xmid = dscr("xmid", (S, D), F32)
    ab_sp = dscr("ab_sp", (E_ * R, D))
    bb_sp = dscr("bb_sp", (D, E_ * R))

    ident32 = nc.inline_tensor(np.eye(32, dtype=np.float32), name="ident32").ap()
    ident8 = nc.inline_tensor(np.eye(E_, dtype=np.float32), name="ident8").ap()
    ones_f32_128 = nc.inline_tensor(np.ones((128, 128), dtype=np.float32), name="ones128").ap()
    one_f32 = nc.inline_tensor(np.ones((1, 1), dtype=np.float32), name="one11").ap()
    ones_col_bf = nc.inline_tensor(np.ones((128, 1), dtype=BF), name="onescol").ap()
    ones_col_f32 = nc.inline_tensor(np.ones((128, 1), dtype=np.float32), name="onescolf").ap()
    ones_bf_128 = nc.inline_tensor(np.ones((128, 128), dtype=BF), name="onesbf128").ap()
    expm = np.zeros((E_, E_ * R), dtype=np.float32)
    for e in range(E_):
        expm[e, e * R:(e + 1) * R] = 1.0
    exp_d = nc.inline_tensor(expm, name="expmat").ap()
    repm = np.zeros((HKV, 128), dtype=np.float32)
    for kv in range(HKV):
        repm[kv, kv * REP:(kv + 1) * REP] = 1.0
    rep_d = nc.inline_tensor(repm, name="repmat").ap()
    mcols = np.zeros((128, NT), dtype=np.float32)
    for t in range(QS_, QE_):
        mcols[t % 128, t // 128] = 1.0 / nmask
    mask_d = nc.inline_tensor(mcols.astype(BF), name="maskcols").ap()

    stack = ExitStack()
    bld = Bld(nc, stack)

    def sb(name, shape, dt):
        return stack.enter_context(nc.sbuf_tensor(name, list(shape), dt))

    arena = sb("arena", (128, DC * S), BF16)

    def ar(dc):
        return arena[:, dc * S:(dc + 1) * S]

    def arb(dc):
        return ("arena", dc)

    qs_cols = sb("qs_cols", (128, DC), F32)
    ks_cols = sb("ks_cols", (128, KV // 128), F32)
    vs_cols = sb("vs_cols", (128, KV // 128), F32)
    os_cols = sb("os_cols", (128, DC), F32)
    b_cols = sb("b_cols", (128, DC), F32)
    sv_all = sb("sv_all", (128, KV // 128), F32)
    vsc_all = sb("vsc_all", (128, KV // 128), F32)
    ks_ev = sb("ks_ev", (128, KV // 128), F32)
    qs_ev = sb("qs_ev", (128, DC), F32)
    os_ev = sb("os_ev", (128, DC), F32)
    qcols = sb("qcols", (128, DC), F32)
    mask_sb = sb("mask_sb", (128, NT), BF16)
    qmax = sb("qmax", (1, 128), F32)
    kmax = sb("kmax", (1, 128), F32)
    qcm = sb("qcm", (128, 32), F32)
    kcm = sb("kcm", (128, 8), F32)
    nm_row = sb("nm_row", (1, 128), F32)
    nm_cols = sb("nm_cols", (128, 128), F32)
    route_er = sb("route_er", (128, 1), F32)
    stage = sb("stage", (128, 8 * 512), BF16)
    small = sb("small", (128, 64), F32)
    lnstats = sb("lnstats", (128, 96), F32)
    tiny = sb("tiny", (1, 64), F32)
    ones128_sb = sb("ones128_sb", (128, 128), F32)
    one_sb = sb("one_sb", (1, 1), F32)
    eps_col = sb("eps_col", (128, 1), F32)
    onescol_sb = sb("onescol_sb", (128, 1), BF16)
    onesbf_sb = sb("onesbf_sb", (128, 128), BF16)
    onescol_f32 = sb("onescol_f32", (128, 1), F32)
    id32_sb = sb("id32_sb", (32, 32), F32)
    id8_sb = sb("id8_sb", (E_, E_), F32)
    exp_sb = sb("exp_sb", (E_, E_ * R), F32)
    rep_sb = sb("rep_sb", (HKV, 128), F32)
    scl_tmp = sb("scl_tmp", (32, 128), F32)

    psA = stack.enter_context(nc.psum_tensor("psA", [128, 2048], F32))
    psB = stack.enter_context(nc.psum_tensor("psB", [128, 2048], F32))

    def bank(i, w=512):
        t = psA if i < 4 else psB
        return t[:, (i % 4) * 512:(i % 4) * 512 + w]

    def bankb(i):
        return ("ps", i)

    _tq = [0]

    def qT():
        return "qS"

    def mm(out_ap, lhsT, rhs, reads, writes, start=True, stop=True):
        bld.op("pe", lambda t, o=out_ap, l=lhsT, r=rhs, st=start, sp=stop:
               t.matmul(o, l, r, start=st, stop=sp), reads=reads, writes=writes)

    for dst, src, buf in (
        (ones128_sb, ones_f32_128, "c_ones128"), (one_sb, one_f32, "c_one"),
        (onescol_sb, ones_col_bf, "c_onescol"), (onesbf_sb, ones_bf_128, "c_onesbf"),
        (onescol_f32, ones_col_f32, "c_onescolf"),
        (id32_sb, ident32, "c_id32"),
        (id8_sb, ident8, "c_id8"), (exp_sb, exp_d, "c_exp"),
        (rep_sb, rep_d, "c_rep"), (mask_sb, mask_d, "c_mask"),
    ):
        bld.dma("qS", dst[:], src[:], writes=[buf])

    bld.op("dve", lambda v, o=qmax[:]: v.memset(o, 0.0), writes=["qmax_all"])
    bld.op("dve", lambda v, o=kmax[:]: v.memset(o, 0.0), writes=["kmax_all"])
    bld.op("dve", lambda v, o=eps_col[:]: v.memset(o, 1e-5), writes=["eps_col"])

    def load_scale_cols(src_ap, n, dst, dstbuf, scale_mult=None):
        assert n <= 32
        tb = ("scl_tmp",)
        bld.dma("qS", scl_tmp[0:n, :], src_ap.rearrange("(a b) -> a b", b=128),
                writes=[tb])
        pb = bankb(7)
        bld.op("pe", lambda t, o=bank(7, n), i=scl_tmp[0:n, :], idn=id32_sb[0:n, 0:n]:
               t.transpose(o, i, idn), reads=[tb, "c_id32"], writes=[pb])
        if scale_mult is None:
            bld.op("dve", lambda v, o=dst[:, 0:n], i=bank(7, n): v.tensor_copy(o, i),
                   reads=[pb], writes=[dstbuf])
        else:
            bld.op("dve", lambda v, o=dst[:, 0:n], i=bank(7, n), m=scale_mult:
                   v.tensor_scalar(o, i, m, None, ALU.mult), reads=[pb], writes=[dstbuf])

    load_scale_cols(qs_d, DC, qs_cols, "qs_cols", scale_mult=float(1.0 / math.sqrt(DH)))
    load_scale_cols(ks_d, KV // 128, ks_cols, "ks_cols")
    load_scale_cols(vs_d, KV // 128, vs_cols, "vs_cols")
    load_scale_cols(os_d, DC, os_cols, "os_cols")
    load_scale_cols(bf_d, DC, b_cols, "b_cols")

    NBS = D // 512

    def layer_norm_tile(xbuf, xb, lnbuf, lnb, col):
        par = (col // 8) % 2
        stats = lnstats[:, par * 48:par * 48 + 6 * NBS]
        mv = small[:, col:col + 2]
        rstd = small[:, col + 4:col + 5]; nmr = small[:, col + 5:col + 6]
        c = ("lncol", col)
        for j in range(NBS):
            bld.op("dve", lambda v, o=lnstats[:, par * 48 + 6 * j:par * 48 + 6 * j + 6],
                   i=xbuf[:, j * 512:(j + 1) * 512]: v.bn_stats(o, i),
                   reads=[xb], writes=[c])
        bld.op("dve", lambda v, o=mv, i=stats: v.bn_aggr(o, i), reads=[c], writes=[c])
        bld.op("act", lambda s, o=rstd, i=mv[:, 1:2], b=eps_col[:]:
               s.activation(o, i, AF.Sqrt, bias=b), reads=[c, "eps_col"], writes=[c])
        bld.op("dve", lambda v, o=rstd, i=rstd: v.reciprocal(o, i), reads=[c], writes=[c])
        bld.op("dve", lambda v, o=nmr, i=mv[:, 0:1], j=rstd: v.scalar_tensor_tensor(
            o, i, -1.0, j, ALU.mult, ALU.mult), reads=[c], writes=[c])
        bld.op("act", lambda s, o=lnbuf[:], i=xbuf[:], sc=rstd, b=nmr:
               s.activation(o, i, AF.Identity, bias=b, scale=sc),
               reads=[xb, c], writes=[lnb])

    wbufs = {}
    prep_q = []

    def open_prep_bufs(st, sfx, nbuf=2):
        wbufs["nb"] = nbuf
        wbufs["wld"] = [st.enter_context(nc.sbuf_tensor(f"wld{sfx}{i}", [128, D], BF16))
                        for i in range(nbuf)]
        wbufs["sgn"] = [st.enter_context(nc.sbuf_tensor(f"sgn{sfx}{i}", [128, D], BF16))
                        for i in range(nbuf)]

    def pump(n):
        for _ in range(n):
            if prep_q:
                prep_q.pop(0)[1]()

    def pump_label(label):
        while any(l == label for l, _ in prep_q):
            prep_q.pop(0)[1]()

    def queue_prep(w_dram, sgn_dram, O, wtag, scl_cols=None, sclbuf=None,
                   ev_dst=None, evbuf=None, s_dst=None, s_dstbuf=None):
        def prep_chunk(fc):
            nb = wbufs["nb"]
            wld = wbufs["wld"][fc % nb]; sgn_sb = wbufs["sgn"][fc % nb]
            wb = ("wld", fc % nb)
            bld.dma("qP", wld[:], w_dram[fc * 128:(fc + 1) * 128, :], writes=[wb])
            sc = ("sprep", fc % 4)
            srow = small[:, 16 + (fc % 4):17 + (fc % 4)]
            bld.op("dve", lambda v, o=srow, i_=wld[:]: v.tensor_reduce(
                o, i_, AX.X, ALU.add, apply_absolute_value=True),
                reads=[wb], writes=[sc])
            bld.op("dve", lambda v, o=srow, i_=srow: v.tensor_scalar(
                o, i_, 1.0 / D, 1e-8, ALU.mult, ALU.max), reads=[sc], writes=[sc])
            if scl_cols is not None:
                bld.op("dve", lambda v, o=ev_dst[:, fc:fc + 1], i_=srow,
                       j=scl_cols[:, fc:fc + 1]: v.tensor_mul(o, i_, j),
                       reads=[sc, sclbuf], writes=[(evbuf, fc)])
            elif s_dst is not None:
                bld.op("dve", lambda v, o=s_dst[:, fc:fc + 1], i_=srow:
                       v.tensor_copy(o, i_), reads=[sc], writes=[(s_dstbuf, fc)])
            sg = ("sgn_sb", fc % nb)
            bld.op("act", lambda s, o=sgn_sb[:], i_=wld[:]:
                   s.activation(o, i_, AF.Sign), reads=[wb], writes=[sg])
            bld.dma("qA", sgn_dram[fc * 128:(fc + 1) * 128, :], sgn_sb[:],
                    reads=[sg], writes=[(wtag, fc)])
        for fc in range(O // 128):
            prep_q.append((wtag, lambda fc=fc: prep_chunk(fc)))

    def queue_tpass(sgn_dram, sgnT_dram, O, wtag):
        nall = [(wtag, fc) for fc in range(O // 128)]

        def t_chunk(dc):
            nb = wbufs["nb"]
            sgn_sb = wbufs["sgn"][dc % nb]
            sg = ("sgn_sb", dc % nb)
            bld.dma(qT(), sgn_sb[:, 0:O], sgn_dram[0:O, dc * 128:(dc + 1) * 128],
                    reads=nall, writes=[sg], transpose=True)
            bld.dma("qP", sgnT_dram[dc * 128:(dc + 1) * 128, 0:O], sgn_sb[:, 0:O],
                    reads=[sg], writes=[(wtag, "T", dc)])
        for dc in range(DC):
            prep_q.append((wtag, lambda dc=dc: t_chunk(dc)))

    def queue_cast(w_dram, cast_dram, O, wtag):
        def c_chunk(fc):
            bld.dma("qP", cast_dram[fc * 128:(fc + 1) * 128, :],
                    w_dram[fc * 128:(fc + 1) * 128, :], writes=[(wtag, fc)])
        for fc in range(O // 128):
            prep_q.append((wtag, lambda fc=fc: c_chunk(fc)))

    queue_prep(wk_d, sgnk, KV, "sgnk", scl_cols=ks_cols, sclbuf="ks_cols",
               ev_dst=ks_ev, evbuf="ks_ev")
    queue_prep(wv_d, sgnv, KV, "sgnv", s_dst=sv_all, s_dstbuf="sv_all")
    queue_prep(wq_d, sgnq, D, "sgnq", scl_cols=qs_cols, sclbuf="qs_cols",
               ev_dst=qs_ev, evbuf="qs_ev")
    queue_prep(wo_d, sgno, D, "sgno", scl_cols=os_cols, sclbuf="os_cols",
               ev_dst=os_ev, evbuf="os_ev")
    queue_cast(wf_d, wfb, D, "wfb")

    pa = ExitStack()

    def pal(name, shape, dt):
        return pa.enter_context(nc.sbuf_tensor(name, list(shape), dt))

    xt = [pal(f"xt{i}", (128, D), F32) for i in range(2)]
    lnt = [stage[:, 0:D]]
    open_prep_bufs(pa, "0")

    for tt in range(NT):
        i = tt % 2
        xb = ("xt", i); lnb = ("lnt", 0)
        bld.dma("qS", xt[i][:], x_d[tt * 128:(tt + 1) * 128, :], writes=[xb])
        layer_norm_tile(xt[i], xb, lnt[0], lnb, col=8 * i)
        bld.dma("qA", ln1_tm[tt * 128:(tt + 1) * 128, :], lnt[0], reads=[lnb],
                writes=[("ln1_tm", tt)])
        pump(1)

    HS = S // 2
    for hh in range(2):
        rt_h = [("ln1_tm", t) for t in range(hh * (NT // 2), (hh + 1) * (NT // 2))]
        for dc in range(DC):
            bld.dma(qT(), ar(dc)[:, hh * HS:(hh + 1) * HS],
                    ln1_tm[hh * HS:(hh + 1) * HS, dc * 128:(dc + 1) * 128],
                    reads=rt_h, writes=[arb(dc)], transpose=True)

    bld.barrier()
    pa.close()

    pw = ExitStack()

    def pwl(name, shape, dt):
        return pw.enter_context(nc.sbuf_tensor(name, list(shape), dt))

    def open_proj_bufs(st, sfx):
        wbufs["wT"] = st.enter_context(nc.sbuf_tensor(f"wT{sfx}", [128, DC * FW], BF16))

    open_proj_bufs(pw, "A")

    def proj(stage_dram, O, out_dram, wtag, out_tag, pump_rate=0,
             ev_scales=None, evbuf=None, ev_bias=None, evb_buf=None,
             norm_dst=None, extra_mm=None, g_outer=False, post_gp=None,
             fb_hook=None, pre_mm=None, staged=False):
        wT = wbufs["wT"]
        FWo = min(FW, O)
        nfb = O // FWo
        nfi = FWo // 128
        assert nfi * GSZ <= 8
        pump_label(wtag)
        for sw in range(GP if g_outer else 1):
          for fb in range(nfb):
            fcs = [fb * nfi + fi for fi in range(nfi)]
            for dc in range(DC):
                if staged:
                    bld.dma("qS", wT[:, dc * FW:dc * FW + FWo],
                            stage_dram[dc * 128:(dc + 1) * 128, fb * FWo:(fb + 1) * FWo],
                            reads=[(wtag, "T", dc)],
                            writes=[("wT", dc)])
                else:
                    bld.dma(qT(), wT[:, dc * FW:dc * FW + FWo],
                            stage_dram[fb * FWo:(fb + 1) * FWo, dc * 128:(dc + 1) * 128],
                            reads=[(wtag, fc) for fc in fcs],
                            writes=[("wT", dc)], transpose=True)
            if pre_mm is not None:
                pre_mm(sw, fb)
            for gp in ([sw] if g_outer else range(GP)):
                for dc in range(DC):
                    for fi in range(nfi):
                        for g2 in range(GSZ):
                            g = gp * GSZ + g2
                            bi = fi * GSZ + g2
                            mm(bank(bi, TG),
                               wT[:, dc * FW + fi * 128: dc * FW + fi * 128 + 128],
                               ar(dc)[:, g * TG:(g + 1) * TG],
                               reads=[("wT", dc), arb(dc)], writes=[bankb(bi)],
                               start=(dc == 0),
                               stop=(dc == DC - 1 and extra_mm is None))
                if extra_mm is not None:
                    for fi in range(nfi):
                        for g2 in range(GSZ):
                            extra_mm(fcs[fi], gp * GSZ + g2, bank(fi * GSZ + g2, TG),
                                     bankb(fi * GSZ + g2))
                for fi in range(nfi):
                    fc = fcs[fi]
                    for g2 in range(GSZ):
                        g = gp * GSZ + g2
                        bi = fi * GSZ + g2
                        stg = stage[:, bi * 512:bi * 512 + TG]
                        stgb = ("stage", bi)
                        kw = {}
                        rd = [bankb(bi)]
                        if ev_scales is not None:
                            kw["scale"] = ev_scales[:, fc:fc + 1]
                            rd.append((evbuf, fc))
                        if ev_bias is not None:
                            kw["bias"] = ev_bias[:, fc:fc + 1]
                            rd.append((evb_buf, fc))
                        bld.op("act", lambda s, o=stg, i=bank(bi, TG), kw=kw:
                               s.activation(o, i, AF.Identity, **kw),
                               reads=rd, writes=[stgb])
                        bld.dma("qA",
                                out_dram[fc * 128:(fc + 1) * 128, g * TG:(g + 1) * TG],
                                stg, reads=[stgb], writes=[(out_tag, fc, g)])
                        if norm_dst is not None:
                            dst, dtag = norm_dst
                            if g == 0:
                                bld.op("dve", lambda v, o=dst[:, fc:fc + 1], i=stg:
                                       v.tensor_reduce(o, i, AX.X, ALU.max,
                                                       apply_absolute_value=True),
                                       reads=[stgb], writes=[(dtag, fc)])
                            else:
                                red = small[:, 56 + bi:57 + bi]
                                redb = ("smallred", bi)
                                bld.op("dve", lambda v, o=red, i=stg:
                                       v.tensor_reduce(o, i, AX.X, ALU.max,
                                                       apply_absolute_value=True),
                                       reads=[stgb], writes=[redb])
                                bld.op("dve", lambda v, o=dst[:, fc:fc + 1], i=red,
                                       oo=dst[:, fc:fc + 1]: v.tensor_max(o, i, oo),
                                       reads=[redb, (dtag, fc)], writes=[(dtag, fc)])
            pump(pump_rate)
            if fb_hook is not None:
                fb_hook(sw, fb)
          if g_outer and post_gp is not None:
            post_gp(sw)

    open_prep_bufs(pw, "A")
    proj(sgnk, KV, kt_sp, "sgnk", "kt", pump_rate=10,
         ev_scales=ks_ev, evbuf="ks_ev", norm_dst=(kcm, "kcm"))

    proj(sgnv, KV, vt_sp, "sgnv", "vt", pump_rate=10)
    for c in range(KV // 128):
        bld.op("dve", lambda v, o=vsc_all[:, c:c + 1], i=sv_all[:, c:c + 1],
               j=vs_cols[:, c:c + 1]: v.tensor_mul(o, i, j),
               reads=[("sv_all", c), "vs_cols"], writes=[("vsc_all", c)])

    proj(sgnq, D, qt_sp, "sgnq", "qt", pump_rate=6,
         ev_scales=qs_ev, evbuf="qs_ev", norm_dst=(qcm, "qcm"))

    bld.op("dve", lambda v, o=qcm[:, 0:H], i=qcm[:, 0:H]: v.tensor_mul(o, i, i),
           reads=[("qcm", c) for c in range(H)],
           writes=[("qcm", c) for c in range(H)])
    bld.op("dve", lambda v, o=kcm[:, 0:HKV], i=kcm[:, 0:HKV]: v.tensor_mul(o, i, i),
           reads=[("kcm", c) for c in range(HKV)],
           writes=[("kcm", c) for c in range(HKV)])
    bld.op("pe", lambda t, o=bank(5)[0:H, 0:1], l=qcm[:, 0:H], r=onescol_f32[:]:
           t.matmul(o, l, r, start=True, stop=True),
           reads=[("qcm", c) for c in range(H)] + ["c_onescolf"], writes=[bankb(5)])
    bld.op("pe", lambda t, o=bank(6)[0:HKV, 0:1], l=kcm[:, 0:HKV], r=onescol_f32[:]:
           t.matmul(o, l, r, start=True, stop=True),
           reads=[("kcm", c) for c in range(HKV)] + ["c_onescolf"], writes=[bankb(6)])
    bld.op("pe", lambda t, o=bank(7)[0:1, 0:H], l=bank5_sb[0:H, 0:1], r=id32_sb[0:H, 0:H]:
           t.matmul(o, l, r, start=True, stop=True),
           reads=["bank5_sb", "c_id32"], writes=[bankb(7)]) if False else None
    qsumc = small[0:H, 40:41]
    ksumc = small[0:HKV, 42:43]
    bld.op("dve", lambda v, o=qsumc, i=bank(5)[0:H, 0:1]: v.tensor_copy(o, i),
           reads=[bankb(5)], writes=["qsumc"])
    bld.op("dve", lambda v, o=ksumc, i=bank(6)[0:HKV, 0:1]: v.tensor_copy(o, i),
           reads=[bankb(6)], writes=["ksumc"])
    bld.op("pe", lambda t, o=bank(5)[0:1, 0:H], l=qsumc, r=id32_sb[0:H, 0:H]:
           t.matmul(o, l, r, start=True, stop=True),
           reads=["qsumc", "c_id32"], writes=[bankb(5)])
    bld.op("dve", lambda v, o=qmax[0:1, 0:H], i=bank(5)[0:1, 0:H]: v.tensor_copy(o, i),
           reads=[bankb(5)], writes=["qmax_all"])
    bld.op("pe", lambda t, o=bank(6)[0:1, 0:HKV], l=ksumc, r=id32_sb[0:HKV, 0:HKV]:
           t.matmul(o, l, r, start=True, stop=True),
           reads=["ksumc", "c_id32"], writes=[bankb(6)])
    bld.op("dve", lambda v, o=kmax_col2, i=bank(6)[0:1, 0:HKV]: v.tensor_copy(o, i),
           reads=[bankb(6)], writes=["kmax_col"]) if False else None
    bld.op("dve", lambda v, o=kmax[0:1, 0:HKV], i=bank(6)[0:1, 0:HKV]: v.tensor_copy(o, i),
           reads=[bankb(6)], writes=["kmax_all"])
    kmax_col = small[0:HKV, 44:45]
    bld.op("pe", lambda t, o=bank(6)[0:HKV, 0:1], l=kmax[0:1, 0:HKV], r=one_sb[:]:
           t.matmul(o, l, r, start=True, stop=True),
           reads=["kmax_all", "c_one"], writes=[bankb(6)])
    bld.op("dve", lambda v, o=kmax_col, i=bank(6)[0:HKV, 0:1]: v.tensor_copy(o, i),
           reads=[bankb(6)], writes=["kmax_colb"])
    bld.op("pe", lambda t, o=bank(6)[0:1, 0:128], l=kmax_col, r=rep_sb[:]:
           t.matmul(o, l, r, start=True, stop=True),
           reads=["kmax_colb", "c_rep"], writes=[bankb(6)])
    bld.op("dve", lambda v, o=nm_row[0:1, 0:H], i=qmax[0:1, 0:H],
           j=bank(6)[0:1, 0:H]: v.tensor_mul(o, i, j),
           reads=[bankb(6), "qmax_all"], writes=["nm_row"])
    bld.op("act", lambda s, o=nm_row[0:1, 0:H], i=nm_row[0:1, 0:H]:
           s.activation(o, i, AF.Sqrt), reads=["nm_row"], writes=["nm_row"])
    bld.op("dve", lambda v, o=nm_row[0:1, 0:H], i=nm_row[0:1, 0:H]: v.tensor_scalar(
        o, i, -1.0, None, ALU.mult), reads=["nm_row"], writes=["nm_row"])
    bld.op("pe", lambda t, o=bank(6)[:, 0:H], l=ones128_sb[0:1, :], r=nm_row[0:1, 0:H]:
           t.matmul(o, l, r, start=True, stop=True),
           reads=["nm_row", "c_ones128"], writes=[bankb(6)])
    bld.op("dve", lambda v, o=nm_cols[:, 0:H], i=bank(6)[:, 0:H]: v.tensor_copy(o, i),
           reads=[bankb(6)], writes=["nm_cols"])

    bld.barrier()
    pw.close()

    pb = ExitStack()

    def pbl(name, shape, dt):
        return pb.enter_context(nc.sbuf_tensor(name, list(shape), dt))

    open_prep_bufs(pb, "B")
    bld.dma("qP", ab_sp[:, :], a_d[:, :], writes=[("ab_sp",)])
    for e in range(E_):
        for oc in range(D // 128):
            bld.dma("qP", bb_sp[oc * 128:(oc + 1) * 128, e * R:(e + 1) * R],
                    b_d[e, oc * 128:(oc + 1) * 128, :], writes=[("bb", e, oc)])
    ktr = [pbl(f"ktr{i}", (128, S), BF16) for i in range(2)]
    vtm = [pbl(f"vtm{i}", (128, NT * 128), BF16) for i in range(2)]
    qh = [pbl(f"qh{i}", (128, QW), BF16) for i in range(4)]
    pt = [pbl(f"pt{i}", (128, 2 * QW), BF16) for i in range(2)]
    rcp = [pbl("rcp0", (128, QW), F32)] * 2
    assert REP % 2 == 0 or REP == 1

    def load_kv(kv, ib):
        bld.dma("qP", ktr[ib][:], kt_sp[kv * 128:(kv + 1) * 128, :],
                reads=[("kt", kv, g) for g in range(G)], writes=[("ktr", ib)])
        for tt in range(NT):
            bld.dma(qT(), vtm[ib][:, tt * 128:(tt + 1) * 128],
                    vt_sp[kv * 128:(kv + 1) * 128, tt * 128:(tt + 1) * 128],
                    reads=[("vt", kv, (tt * 128) // TG)],
                    writes=[("vtm", ib)], transpose=True)

    npair = max(1, REP // 2)
    ctxs = [(kv, qp, pr) for kv in range(HKV) for qp in range(QP)
            for pr in range(npair)]

    def ctx_heads(ci):
        kv, qp, pr = ctxs[ci]
        hA = kv * REP + 2 * pr
        return [hA, hA + 1] if REP > 1 else [hA]

    def qh_load(ci):
        kv, qp, pr = ctxs[ci]
        for x, h in enumerate(ctx_heads(ci)):
            bi = (ci % 2) * 2 + x
            bld.dma("qS", qh[bi][:],
                    qt_sp[h * 128:(h + 1) * 128, qp * QW:(qp + 1) * QW],
                    reads=[("qt", h, (qp * QW) // TG)], writes=[("qh", bi)])

    qh_load(0)
    for ci, (kv, qp, pr) in enumerate(ctxs):
        ib = kv % 2
        if qp == 0 and pr == 0:
            if kv == 0:
                load_kv(0, 0)
            if kv + 1 < HKV:
                load_kv(kv + 1, (kv + 1) % 2)
        if ci + 1 < len(ctxs):
            qh_load(ci + 1)
        if True:
            if True:
                hs = ctx_heads(ci)

                def st_mm(x, kc):
                    bk = x * 2 + (kc % 2)
                    bi = (ci % 2) * 2 + x
                    mm(bank(bk, QW), ktr[ib][:, kc * 128:(kc + 1) * 128],
                       qh[bi][:], reads=[("ktr", ib), ("qh", bi)], writes=[bankb(bk)])

                def exp_op(x, kc):
                    bk = x * 2 + (kc % 2)
                    h = hs[x]
                    po = (kc % 2) * QW
                    bld.op("act", lambda s, o=pt[x][:, po:po + QW], i=bank(bk, QW),
                           b=nm_cols[:, h:h + 1]: s.activation(o, i, AF.Exp, bias=b),
                           reads=[bankb(bk), "nm_cols"], writes=[("pt", x, kc % 2)])

                def pv_mm(x, kc):
                    po = (kc % 2) * QW
                    mm(bank(4 + x, QW), vtm[ib][:, kc * 128:(kc + 1) * 128],
                       pt[x][:, po:po + QW],
                       reads=[("vtm", ib), ("pt", x, kc % 2)], writes=[bankb(4 + x)],
                       start=(kc == 0), stop=(kc == KC - 1))

                def den_mm(x, kc):
                    po = (kc % 2) * QW
                    mm(bank(6 + x, QW), onesbf_sb[:], pt[x][:, po:po + QW],
                       reads=[("pt", x, kc % 2), "c_onesbf"], writes=[bankb(6 + x)],
                       start=(kc == 0), stop=(kc == KC - 1))

                for kc in range(KC):
                    for x in range(len(hs)):
                        st_mm(x, kc)
                    if kc >= 1:
                        for x in range(len(hs)):
                            pv_mm(x, kc - 1)
                            den_mm(x, kc - 1)
                    for x in range(len(hs)):
                        exp_op(x, kc)
                for x in range(len(hs)):
                    pv_mm(x, KC - 1)
                    den_mm(x, KC - 1)
                for x, h in enumerate(hs):
                    bld.op("dve", lambda v, o=rcp[x][:], i=bank(6 + x, QW):
                           v.reciprocal(o, i), reads=[bankb(6 + x)], writes=[("rcp", 0)])
                    bld.op("dve",
                           lambda v, o=ar(h)[:, qp * QW:(qp + 1) * QW],
                           i=bank(4 + x, QW), sc=vsc_all[:, kv:kv + 1], jj=rcp[x][:]:
                           v.scalar_tensor_tensor(o, i, sc, jj, ALU.mult, ALU.mult),
                           reads=[bankb(4 + x), ("rcp", 0), ("vsc_all", kv)],
                           writes=[arb(h)])
                pump(2)

    pump(len(prep_q))
    bld.barrier()
    pb.close()

    pc = ExitStack()
    open_proj_bufs(pc, "O")

    def pcl(name, shape, dt):
        return pc.enter_context(nc.sbuf_tensor(name, list(shape), dt))

    optm = pcl("optm", (128, D), BF16)
    xt2 = pcl("xt2", (128, D), F32)
    ln2t = pcl("ln2t", (128, D), BF16)

    def c0b_tile(tt):
        ob = ("optm",)
        bld.dma(qT(), optm[:], op_sp[0:D, tt * 128:(tt + 1) * 128],
                reads=[("opj", fc, (tt * 128) // TG) for fc in range(DC)],
                writes=[ob], transpose=True)
        xb = ("xt2",)
        bld.dma("qS", xt2[:], x_d[tt * 128:(tt + 1) * 128, :], writes=[xb])
        bld.op("dve", lambda v, o=xt2[:], a=xt2[:], b=optm[:]:
               v.tensor_add(o, a, b), reads=[xb, ob], writes=[xb])
        bld.dma("qP", xmid[tt * 128:(tt + 1) * 128, :], xt2[:], reads=[xb],
                writes=[("xmid", tt)])
        lnb = ("ln2t",)
        layer_norm_tile(xt2, xb, ln2t, lnb, col=32 + 8 * (tt % 2))
        bld.dma("qA", ln2_tm[tt * 128:(tt + 1) * 128, :], ln2t[:], reads=[lnb],
                writes=[("ln2_tm", tt)])

    tiles_per_gp = GSZ * (TG // 128)

    def emit_router():
        rw_small = scl_tmp[0:E_, :]
        t_first = QS_ // 128
        t_last = (QE_ - 1) // 128
        first = True
        for tt in range(t_first, t_last + 1):
            rb_ = ("optm",)
            bld.dma("qP", optm[:], ln2_tm[tt * 128:(tt + 1) * 128, :],
                    reads=[("ln2_tm", tt)], writes=[rb_])
            for dc in range(DC):
                mm(bank(0)[:, dc:dc + 1], optm[:, dc * 128:(dc + 1) * 128],
                   mask_sb[:, tt:tt + 1], reads=[rb_, "c_mask"], writes=[bankb(0)],
                   start=first, stop=(tt == t_last and dc == DC - 1))
                first = False
        bld.op("dve", lambda v, o=qcols[:, 0:DC], i=bank(0)[:, 0:DC]: v.tensor_copy(o, i),
               reads=[bankb(0)], writes=[("qcols", dc2) for dc2 in range(DC)])

        rwc = small[:, 54:54 + E_]
        for dc in range(DC):
            bld.dma("qS", rw_small, rw_d[:, dc * 128:(dc + 1) * 128], writes=["rw_sb"])
            bld.op("pe", lambda t, o=bank(7)[:, 0:E_], i=rw_small,
                   idn=id8_sb[:]: t.transpose(o, i, idn),
                   reads=["rw_sb", "c_id8"], writes=[bankb(7)])
            bld.op("dve", lambda v, o=rwc, i=bank(7)[:, 0:E_]: v.tensor_copy(o, i),
                   reads=[bankb(7)], writes=["rwc"])
            mm(bank(5)[0:E_, 0:1], rwc, qcols[:, dc:dc + 1],
               reads=["rwc", ("qcols", dc)], writes=[bankb(5)],
               start=(dc == 0), stop=(dc == DC - 1))
        rbrow = tiny[0:1, 0:E_]
        bld.dma("qS", rbrow, rb_d.rearrange("(o e) -> o e", o=1), writes=["rbrow"])
        bld.op("pe", lambda t, o=bank(6)[0:E_, 0:1], l=rbrow, r=one_sb[:]:
               t.matmul(o, l, r, start=True, stop=True), reads=["rbrow", "c_one"],
               writes=[bankb(6)])
        lgc = small[0:E_, 51:52]
        bld.op("dve", lambda v, o=lgc, i=bank(5)[0:E_, 0:1]: v.tensor_copy(o, i),
               reads=[bankb(5)], writes=["lgc"])
        bld.op("dve", lambda v, o=lgc, a=lgc, b=bank(6)[0:E_, 0:1]:
               v.tensor_add(o, a, b), reads=["lgc", bankb(6)], writes=["lgc"])
        bld.op("pe", lambda t, o=bank(6)[0:1, 0:E_], l=lgc, r=id8_sb[:]:
               t.matmul(o, l, r, start=True, stop=True), reads=["lgc", "c_id8"],
               writes=[bankb(6)])
        lgrow = tiny[0:1, 8:8 + E_]
        bld.op("dve", lambda v, o=lgrow, i=bank(6)[0:1, 0:E_]: v.tensor_copy(o, i),
               reads=[bankb(6)], writes=["lgrow"])
        mx = tiny[0:1, 16:17]
        bld.op("dve", lambda v, o=mx, i=lgrow: v.tensor_reduce(o, i, AX.X, ALU.max),
               reads=["lgrow"], writes=["mx"])
        exr = tiny[0:1, 17:17 + E_]
        bld.op("dve", lambda v, o=exr, i=lgrow, s=mx: v.tensor_single_scalar(
            o, i, s, ALU.subtract), reads=["lgrow", "mx"], writes=["exr"])
        bld.op("act", lambda s, o=exr, i=exr: s.activation(o, i, AF.Exp),
               reads=["exr"], writes=["exr"])
        sm = tiny[0:1, 25:26]
        bld.op("dve", lambda v, o=sm, i=exr: v.tensor_reduce(o, i, AX.X, ALU.add),
               reads=["exr"], writes=["sm"])
        bld.op("dve", lambda v, o=sm, i=sm: v.reciprocal(o, i), reads=["sm"], writes=["sm"])
        rrow = tiny[0:1, 26:26 + E_]
        bld.op("dve", lambda v, o=rrow, i=exr, s=sm: v.tensor_scalar(
            o, i, s, LORA_SCALE, ALU.mult, ALU.mult), reads=["exr", "sm"], writes=["rrow"])
        bld.op("pe", lambda t, o=bank(6)[0:E_, 0:1], l=rrow, r=one_sb[:]:
               t.matmul(o, l, r, start=True, stop=True), reads=["rrow", "c_one"],
               writes=[bankb(6)])
        rcol = small[0:E_, 52:53]
        bld.op("dve", lambda v, o=rcol, i=bank(6)[0:E_, 0:1]: v.tensor_copy(o, i),
               reads=[bankb(6)], writes=["rcol"])
        mm(bank(6)[:, 0:1], exp_sb[:], rcol, reads=["c_exp", "rcol"], writes=[bankb(6)])
        bld.op("dve", lambda v, o=route_er[:], i=bank(6)[:, 0:1]: v.tensor_copy(o, i),
               reads=[bankb(6)], writes=["route_er"])



    def mk_hooks(tile_fn, tail=None):
        def fb_hook(sw, fb):
            if sw >= 1 and fb < tiles_per_gp:
                tile_fn((sw - 1) * tiles_per_gp + fb)
        def post_gp(sw):
            if sw == GP - 1:
                for tt in range(sw * tiles_per_gp, (sw + 1) * tiles_per_gp):
                    tile_fn(tt)
                if tail is not None:
                    tail()
        return fb_hook, post_gp

    c0b_fb, c0b_post = mk_hooks(c0b_tile, tail=emit_router)
    proj(sgno, D, op_sp, "sgno", "opj", ev_scales=os_ev, evbuf="os_ev",
         g_outer=True, post_gp=c0b_post, fb_hook=c0b_fb)

    bld.barrier()
    pc.close()

    pe_ = ExitStack()

    def pel(name, shape, dt):
        return pe_.enter_context(nc.sbuf_tensor(name, list(shape), dt))

    bTs = [pel(f"bTs{i}", (128, 512), BF16) for i in range(2)]
    uT = pel("uT", (128, G * 512), BF16)
    wbufs["wT"] = pel("wT3", (128, DC * FW), BF16)
    wbufs.pop("sq", None)
    aTs = [pel(f"aTs{i}", (128, 128), BF16) for i in range(4)]
    fftm = pel("fftm", (128, D), BF16)
    xm2 = pel("xm2", (128, D), F32)

    rt2_all = [("ln2_tm", t) for t in range(NT)]
    for dc in range(DC):
        bld.dma(qT(), ar(dc)[:, :], ln2_tm[0:S, dc * 128:(dc + 1) * 128],
                reads=rt2_all, writes=[arb(dc)], transpose=True)


    for dc in range(DC):
        ai = dc % 4
        bld.dma(qT(), aTs[ai][:], ab_sp[0:E_ * R, dc * 128:(dc + 1) * 128],
                reads=[("ab_sp",)], writes=[("aTs", ai)], transpose=True)
        for g in range(G):
            mm(bank(g, TG), aTs[ai][:], ar(dc)[:, g * TG:(g + 1) * TG],
               reads=[("aTs", ai), arb(dc)], writes=[bankb(g)],
               start=(dc == 0), stop=(dc == DC - 1))
    for g in range(G):
        bld.op("act", lambda s, o=uT[:, g * 512:g * 512 + TG], i=bank(g, TG),
               sc=route_er[:]: s.activation(o, i, AF.Identity, scale=sc),
               reads=[bankb(g), "route_er"], writes=[("uT", g)])

    NFI = min(FW, D) // 128

    def bt_pre_mm(sw, fb):
        bi = fb % 2
        bld.dma(qT(), bTs[bi][:, 0:NFI * 128],
                bb_sp[fb * NFI * 128:(fb + 1) * NFI * 128, 0:E_ * R],
                reads=[("bb", e, oc) for e in range(E_)
                       for oc in range(fb * NFI, (fb + 1) * NFI)],
                writes=[("bTs", bi)], transpose=True)

    def lora_mm(fc, g, bankap, bankbuf):
        bi = (fc // NFI) % 2
        mm(bankap, bTs[bi][:, (fc % NFI) * 128:(fc % NFI + 1) * 128],
           uT[:, g * 512:g * 512 + TG],
           reads=[("bTs", bi), ("uT", g)], writes=[bankbuf], start=False, stop=True)

    def c1e_tile(tt):
        fb_ = ("fftm",)
        bld.dma(qT(), fftm[:], ff_sp[0:D, tt * 128:(tt + 1) * 128],
                reads=[("ff", fc, (tt * 128) // TG) for fc in range(DC)],
                writes=[fb_], transpose=True)
        xb = ("xm2",)
        bld.dma("qS", xm2[:], xmid[tt * 128:(tt + 1) * 128, :],
                reads=[("xmid", tt)], writes=[xb])
        bld.op("dve", lambda v, o=xm2[:], a=xm2[:], b=fftm[:]:
               v.tensor_add(o, a, b), reads=[xb, fb_], writes=[xb])
        bld.dma("qS", out_d[tt * 128:(tt + 1) * 128, :], xm2[:], reads=[xb],
                writes=[("out", tt)])

    c1e_fb, c1e_post = mk_hooks(c1e_tile)
    proj(wfb, D, ff_sp, "wfb", "ff", ev_bias=b_cols, evb_buf="b_cols",
         extra_mm=lora_mm, g_outer=True, post_gp=c1e_post, fb_hook=c1e_fb,
         pre_mm=bt_pre_mm)

    bld.barrier()
    pe_.close()

    bld.finalize()

    with nc.Block() as block:
        bld.run(block)
    stack.close()
    return nc



def make_in_maps(dims, inputs, n=8):
    x = np.ascontiguousarray(np.asarray(inputs["L0_attn_hidden_states"], dtype=np.float32))
    c = lambda k: np.ascontiguousarray(np.asarray(inputs[k], np.float32))
    common = {
        "wq": c("L0_attn_q_weight"), "qs": c("L0_attn_q_scale"),
        "wk": c("L0_attn_k_weight"), "ks": c("L0_attn_k_scale"),
        "wv": c("L0_attn_v_weight"), "vs": c("L0_attn_v_scale"),
        "wo": c("L0_attn_o_weight"), "os": c("L0_attn_o_scale"),
        "wf": c("L0_ffn_W"), "bf": c("L0_ffn_b"),
        "A": np.ascontiguousarray(
            np.asarray(inputs["L0_ffn_A"], np.float32).reshape(
                dims["E"] * dims["R"], dims["D"])),
        "B": c("L0_ffn_B"),
        "rW": c("L0_ffn_router_W"), "rb": c("L0_ffn_router_b"),
    }
    return [dict(common, x=np.ascontiguousarray(x[i])) for i in range(n)]


def kernel(**inputs):
    dims = full_dims()
    nc = build_program(dims)
    in_maps = make_in_maps(dims, inputs)
    res = run_bass_kernel_spmd(nc, in_maps, core_ids=list(range(8)))
    out = np.stack([np.asarray(r["out"], dtype=np.float32) for r in res.results], axis=0)
    return out
